# version 14
# baseline (speedup 1.0000x reference)
import contextlib
import numpy as np
import ml_dtypes

import concourse.bass as bass
import concourse.mybir as mybir
from concourse.bass_utils import run_bass_kernel_spmd

F32 = mybir.dt.float32
BF16 = mybir.dt.bfloat16
AF = mybir.ActivationFunctionType
ALU = mybir.AluOpType
AX = mybir.AxisListType
BF = ml_dtypes.bfloat16

B, T, D = 2, 8192, 2048
NCORE = 8
TOK = 2048
WIN = 512
HK, NREP, HD = 4, 4, 128
H = HK * NREP
KVD = HK * HD
INTER = 5632
NCH = D // 128
NICH = INTER // 128
GT = 1024
NG = TOK // GT
SG = 512
NSG = GT // SG
QSCALE = float(1.0 / np.sqrt(HD))
EPS = 1e-5

ENGS = ("sync", "gpsimd", "tensor", "vector", "scalar")
SPACING = 6
PAGE = 128


class Tok:
    __slots__ = ("eng", "idx", "sem", "val", "is_dma")
    def __init__(self, eng, idx, sem, val, is_dma=False):
        self.eng, self.idx, self.sem, self.val, self.is_dma = eng, idx, sem, val, is_dma


class Page:
    __slots__ = ("w", "r")
    def __init__(self):
        self.w = None
        self.r = []


class Builder:
    def __init__(self):
        self.streams = {e: [] for e in ENGS}
        self.icount = {e: 0 for e in ENGS}
        self.semcnt = {}
        self.maxseen = {e: {} for e in ENGS}
        self.pages = {}

    def sem_next(self, name, n):
        v = self.semcnt.get(name, 0) + n
        self.semcnt[name] = v
        return v

    def page(self, key):
        p = self.pages.get(key)
        if p is None:
            p = Page()
            self.pages[key] = p
        return p

    def _want(self, eng, tok, waits):
        if tok is None:
            return
        if tok.eng == eng and not tok.is_dma:
            if self.icount[eng] - tok.idx >= SPACING:
                return
        if self.maxseen[eng].get(tok.sem, 0) >= tok.val:
            return
        if tok.val > waits.get(tok.sem, 0):
            waits[tok.sem] = tok.val

    def op(self, eng, build, reads=(), writes=(), wwaits=(), sem=None, inc=0,
           is_dma=False, record_reads=(), extra_waits=()):
        waits = {}
        for pk in reads:
            for k in pk:
                self._want(eng, self.page(k).w, waits)
        for pk in list(writes) + list(wwaits):
            for k in pk:
                p = self.page(k)
                self._want(eng, p.w, waits)
                for t in p.r:
                    self._want(eng, t, waits)
        for s, v in extra_waits:
            if self.maxseen[eng].get(s, 0) < v and v > waits.get(s, 0):
                waits[s] = v
        if writes:
            assert inc > 0, "writing op must carry a semaphore increment"

        myidx = self.icount[eng]
        tok = None
        if inc:
            val = self.sem_next(sem, inc)
            tok = Tok(eng, myidx, sem, val, is_dma=is_dma)

        wlist = sorted(waits.items())
        for s, v in wlist:
            if self.maxseen[eng].get(s, 0) < v:
                self.maxseen[eng][s] = v
        semname = sem

        def run(e, semmap):
            for s, v in wlist:
                e.wait_ge(semmap[s], v)
            holder = []
            build(e, holder.append)
            assert len(holder) == 1
            if tok is not None:
                holder[0].then_inc(semmap[semname], inc)

        self.streams[eng].append(run)
        self.icount[eng] += 1
        for pk in writes:
            for k in pk:
                p = self.page(k)
                p.w = tok
                p.r = []
        if tok is not None:
            for pk in list(reads) + list(record_reads):
                for k in pk:
                    self.page(k).r.append(tok)
        return tok


class Region:
    __slots__ = ("handle", "off", "n", "space", "name", "ncol")
    def __init__(self, handle, space, off, n, name, ncol):
        self.handle, self.space, self.off, self.n = handle, space, off, n
        self.name, self.ncol = name, ncol

    def ap(self, a=0, b=None):
        if b is None:
            b = self.n
        return self.handle[:, self.off + a:self.off + b]

    def row0(self, a=0, b=None):
        if b is None:
            b = self.n
        return self.handle[0:1, self.off + a:self.off + b]

    def sap(self, elem_off, dims):
        return bass.AP(self.handle, self.off + elem_off, [[self.ncol, 128]] + dims)

    def pg(self, a=0, b=None):
        if b is None:
            b = self.n
        lo = (self.off + a) // PAGE
        hi = (self.off + b - 1) // PAGE
        return [(self.space, i) for i in range(lo, hi + 1)]


class Alloc:
    def __init__(self, handle, space, total, round_to=PAGE):
        self.handle, self.space, self.total = handle, space, total
        self.round_to = round_to
        self.off = 0
        self.marks = []
        self.peak = 0

    def take(self, n, name):
        r_ = self.round_to
        n = ((n + r_ - 1) // r_) * r_
        r = Region(self.handle, self.space, self.off, n, name, self.total)
        self.off += n
        self.peak = max(self.peak, self.off)
        assert self.off <= self.total, f"{self.space} overflow at {name}: {self.off} > {self.total}"
        return r

    def mark(self):
        self.marks.append(self.off)

    def reset(self):
        self.off = self.marks.pop()


DEBUG = False

def build_program():
    nc = bass.Bass("TRN2", target_bir_lowering=False)

    d_x = nc.dram_tensor("x", [TOK, D], F32, kind="ExternalInput").ap()
    d_xh = nc.dram_tensor("xh", [WIN, D], F32, kind="ExternalInput").ap()
    d_mask = nc.dram_tensor("mask", [128, 16 * 640], BF16, kind="ExternalInput").ap()
    d_ident = nc.dram_tensor("ident", [128, 128], BF16, kind="ExternalInput").ap()
    d_wq = nc.dram_tensor("Wq", [D, D], F32, kind="ExternalInput").ap()
    d_wk = nc.dram_tensor("Wk", [D, KVD], F32, kind="ExternalInput").ap()
    d_wv = nc.dram_tensor("Wv", [D, KVD], F32, kind="ExternalInput").ap()
    d_wo = nc.dram_tensor("Wo", [D, D], F32, kind="ExternalInput").ap()
    d_gw = nc.dram_tensor("gw", [INTER, D], F32, kind="ExternalInput").ap()
    d_uw = nc.dram_tensor("uw", [INTER, D], F32, kind="ExternalInput").ap()
    d_dw = nc.dram_tensor("dw", [D, INTER], F32, kind="ExternalInput").ap()
    d_gs = nc.dram_tensor("gs", [1, INTER], F32, kind="ExternalInput").ap()
    d_us = nc.dram_tensor("us", [1, INTER], F32, kind="ExternalInput").ap()
    d_ds = nc.dram_tensor("ds", [1, D], F32, kind="ExternalInput").ap()
    d_out = nc.dram_tensor("out", [TOK, D], F32, kind="ExternalOutput").ap()
    d_dbg = {}
    if DEBUG:
        for g_ in range(2):
            for nm, n in [("xnT", NCH * GT), ("kT", 4 * 1536), ("Vl", 12 * KVD),
                          ("qT", H * GT), ("att", NCH * GT), ("ybuf", 8 * D),
                          ("R2", 8 * D), ("hdnT", NICH * SG), ("y2", 4 * D)]:
                d_dbg[f"{nm}{g_}"] = nc.dram_tensor(f"dbg_{nm}{g_}", [128, n], BF16, kind="ExternalOutput").ap()
    d_guwT = nc.dram_tensor("guwT_s", [NICH, 2, 128, D], BF16).ap()
    d_dwT = nc.dram_tensor("dwT_s", [NCH, 128, NICH * 128], BF16).ap()

    bld = Builder()
    st = contextlib.ExitStack()

    BF_TOTAL = 95360
    F32_TOTAL = 4992
    h_bf = st.enter_context(nc.sbuf_tensor("abf", [128, BF_TOTAL], BF16))
    h_f = st.enter_context(nc.sbuf_tensor("af32", [128, F32_TOTAL], F32))
    A = Alloc(h_bf, "bf", BF_TOTAL)
    Fa = Alloc(h_f, "f32", F32_TOTAL, round_to=32)

    slotA = A.take(8 * D, "slotA")
    slotB = A.take(8 * D, "slotB")
    featA = A.take(NCH * GT, "featA")
    kT = A.take(4 * 1536, "kT")
    Vl = A.take(12 * KVD, "Vl")
    ident_b = A.take(128, "ident")

    A.mark()
    qT = A.take(H * GT, "qT")
    wkb = A.take(NCH * KVD, "wkb")
    wvb = A.take(NCH * KVD, "wvb")
    A.reset(); A.mark()
    hdnT = A.take(NICH * SG, "hdnT")
    fw_stg = [A.take(D, f"fwstg{i}") for i in range(4)]
    A.reset(); A.mark()
    _p1 = A.take(NICH * SG, "_p1")
    dwl_stg = [A.take(NICH * 128, f"dwl{i}") for i in range(2)]
    A.reset(); A.mark()
    _p2 = A.take(H * GT, "_p2")
    masks = A.take(8 * 640, "masks")
    p_st = [A.take(640, f"p_st{i}") for i in range(2)]
    p_m = [A.take(640, f"p_m{i}") for i in range(2)]
    p_n = [A.take(640, f"p_n{i}") for i in range(2)]
    ptk = [A.take(640, f"ptk{i}") for i in range(2)]
    wob = [A.take(NCH * 128, f"wob{i}") for i in range(2)]
    ytr = [A.take(512, f"ytr{i}") for i in range(2)]
    sgsb = [A.take(512, f"sgsb{i}") for i in range(2)]
    A.reset(); A.mark()
    _p2b = A.take(H * GT, "_p2b")
    ynst = [A.take(D, f"ynst{i}") for i in range(2)]
    A.reset(); A.mark()
    dump4_v = A.take(D, "dump4_v")
    dump4_s = A.take(D, "dump4_s")
    A.reset(); A.mark()
    ybuf = A.take(8 * D, "ybuf")
    A.reset(); A.mark()
    w_stg = [A.take(2 * D, f"wstg{i}") for i in range(2)]
    wT_stg = [A.take(D, f"wTstg{i}") for i in range(2)]
    A.reset(); A.mark()
    dw_stg = [A.take(INTER, f"dwstg{i}") for i in range(2)]
    dwT_stg = [A.take(INTER, f"dwTstg{i}") for i in range(2)]
    A.reset()
    Z0 = qT.off
    def zreg(off, n, name):
        return Region(h_bf, "bf", Z0 + off, n, name, BF_TOTAL)
    dumpC_v = zreg(4096, D, "dumpC_v"); dumpC_s = zreg(4096 + D, D, "dumpC_s")
    xnst = [zreg(8192, D, "xn_a"), zreg(8192 + D, D, "xn_b")]
    dumpA_v = zreg(12288, D, "dumpA_v"); dumpA_s = zreg(12288 + D, D, "dumpA_s")
    wqb = [zreg(21504, D, "wqb0"), zreg(21504 + D, D, "wqb1")]
    dumpB_v = zreg(21504, D, "dumpB_v"); dumpB_s = zreg(21504 + D, D, "dumpB_s")

    stat_sum = Fa.take(24, "sum"); stat_sumsq = Fa.take(24, "sumsq")
    stat_mean = Fa.take(24, "mean"); stat_msq = Fa.take(24, "msq")
    stat_var = Fa.take(24, "var"); stat_lnv = Fa.take(24, "lnv")
    stat_rstd = Fa.take(24, "rstd"); stat_nb = Fa.take(24, "nb")
    c_eps = Fa.take(1, "eps"); c_one = Fa.take(1, "one"); c_qs = Fa.take(1, "qs")
    gs_col = Fa.take(NICH, "gs_col"); us_col = Fa.take(NICH, "us_col")
    ds_col = Fa.take(NCH, "ds_col")
    nmax = Fa.take(2, "nmax"); rsum = Fa.take(2, "rsum"); rcp = Fa.take(2, "rcp")
    sc_row = Fa.take(256, "sc_row")
    out_stg = [Fa.take(D, f"ostg{i}") for i in range(2)]

    pT = [st.enter_context(nc.psum_tensor(f"T{i}", [128, 1024], BF16)) for i in range(2)]
    pA = [st.enter_context(nc.psum_tensor(f"A{i}", [128, 512], F32)) for i in range(2)]
    pS = [st.enter_context(nc.psum_tensor(f"S{i}", [128, 1024], F32)) for i in range(2)]

    def ppg(name):
        return [("psum", name)]

    dma_rr = {"sync": 0, "gpsimd": 0}
    def dma(queue, dst_ap, src_ap, rd, wr, **kw):
        base = "dS" if queue == "sync" else "dG"
        sem = f"{base}{dma_rr[queue] % 6}"
        dma_rr[queue] += 1
        prev = bld.semcnt.get(sem, 0)
        def b(e, cb):
            cb(e.dma_start(dst_ap, src_ap, **kw))
        return bld.op(queue, b, reads=rd, writes=wr, sem=sem, inc=16, is_dma=True,
                      extra_waits=[(sem, prev)] if prev else ())

    def vop(meth, *args, rd=(), wr=(), eng="vector", **kw):
        sem = {"vector": "vS", "scalar": "sS", "gpsimd": "gS"}[eng]
        def b(e, cb):
            cb(getattr(e, meth)(*args, **kw))
        return bld.op(eng, b, reads=rd, writes=wr, sem=sem, inc=1)

    def sevac(dst_ap, src_ap, rd, wr, scale=None):
        if scale is None:
            def b(e, cb):
                cb(e.copy(dst_ap, src_ap))
        else:
            def b(e, cb):
                cb(e.activation(dst_ap, src_ap, AF.Identity, scale=scale))
        return bld.op("scalar", b, reads=rd, writes=wr, sem="sS", inc=1)

    def mm_group(items, wr):
        n = len(items)
        allrd = []
        for _, _, _, rdp in items:
            allrd += list(rdp)
        tok = None
        for i, (l, r, o, rdp) in enumerate(items):
            first, last = i == 0, i == n - 1
            def b(e, cb, l=l, r=r, o=o, first=first, last=last):
                cb(e.matmul(o, l, r, start=first, stop=last))
            tok = bld.op("tensor", b, reads=rdp,
                         writes=wr if last else (),
                         wwaits=wr if first else (),
                         sem="tS", inc=1 if last else 0,
                         record_reads=allrd if last else ())
        return tok

    def tr_batch(srcs, psum_name, psum_ap_fn, ident_ap, rd_extra=()):
        n = len(srcs)
        allrd = []
        for _, rdp in srcs:
            allrd += list(rdp)
        tok = None
        for i, (sap, rdp) in enumerate(srcs):
            def b(e, cb, sap=sap, i=i):
                cb(e.transpose(psum_ap_fn(i), sap, ident_ap))
            tok = bld.op("tensor", b, reads=list(rdp) + [ident_b.pg()],
                         writes=[ppg(psum_name)] if i == n - 1 else (),
                         wwaits=[ppg(psum_name)] if i == 0 else (),
                         sem="tS", inc=1 if i == n - 1 else 0,
                         record_reads=allrd if i == n - 1 else ())
        return tok

    def ln_stats_pass1(src, a, b_, col, dv, ds_):
        vop("tensor_scalar", dv.ap(), src.ap(a, b_), 1.0, 0.0, ALU.mult, ALU.add,
            accum_out=stat_sum.ap(col, col + 1),
            rd=[src.pg(a, b_)], wr=[stat_sum.pg(col, col + 1), dv.pg()])
        vop("activation", ds_.ap(), src.ap(a, b_), AF.Square,
            accum_out=stat_sumsq.ap(col, col + 1), eng="scalar",
            rd=[src.pg(a, b_)], wr=[stat_sumsq.pg(col, col + 1), ds_.pg()])

    def ln_stats_pass2(ntile, base=0):
        a, b_ = base, base + ntile
        vop("tensor_scalar_mul", stat_mean.ap(a, b_), stat_sum.ap(a, b_), 1.0 / D,
            rd=[stat_sum.pg(a, b_)], wr=[stat_mean.pg(a, b_)])
        vop("activation", stat_msq.ap(a, b_), stat_mean.ap(a, b_), AF.Square, eng="scalar",
            rd=[stat_mean.pg(a, b_)], wr=[stat_msq.pg(a, b_)])
        vop("scalar_tensor_tensor", stat_var.ap(a, b_), stat_sumsq.ap(a, b_), 1.0 / D,
            stat_msq.ap(a, b_), ALU.mult, ALU.subtract,
            rd=[stat_sumsq.pg(a, b_), stat_msq.pg(a, b_)], wr=[stat_var.pg(a, b_)])
        vop("activation", stat_lnv.ap(a, b_), stat_var.ap(a, b_), AF.Ln, bias=c_eps.ap(0, 1),
            eng="scalar", rd=[stat_var.pg(a, b_), c_eps.pg()], wr=[stat_lnv.pg(a, b_)])
        vop("activation", stat_rstd.ap(a, b_), stat_lnv.ap(a, b_), AF.Exp, scale=-0.5,
            eng="scalar", rd=[stat_lnv.pg(a, b_)], wr=[stat_rstd.pg(a, b_)])
        vop("scalar_tensor_tensor", stat_nb.ap(a, b_), stat_mean.ap(a, b_), -1.0,
            stat_rstd.ap(a, b_), ALU.mult, ALU.mult,
            rd=[stat_mean.pg(a, b_), stat_rstd.pg(a, b_)], wr=[stat_nb.pg(a, b_)])

    def ln_normalize(src, a, b_, col, dst):
        vop("activation", dst.ap(0, b_ - a), src.ap(a, b_), AF.Identity,
            bias=stat_nb.ap(col, col + 1), scale=stat_rstd.ap(col, col + 1), eng="scalar",
            rd=[src.pg(a, b_), stat_nb.pg(col, col + 1), stat_rstd.pg(col, col + 1)],
            wr=[dst.pg(0, b_ - a)])

    def transpose_tile(src, dstT, tokspan, toff):
        for half in range(2):
            tb = pT[half % 2]
            srcs = [(src.ap((half * 8 + c) * 128, (half * 8 + c + 1) * 128),
                     [src.pg((half * 8 + c) * 128, (half * 8 + c + 1) * 128)])
                    for c in range(8)]
            tr_batch(srcs, f"T{half % 2}", lambda i, tb=tb: tb[:, i * 128:(i + 1) * 128], ident_b.ap())
            dst_ap = dstT.sap(half * 8 * tokspan + toff, [[tokspan, 8], [1, 128]])
            wrpg = []
            for c in range(8):
                wrpg += dstT.pg((half * 8 + c) * tokspan + toff, (half * 8 + c) * tokspan + toff + 128)
            vop("tensor_copy", dst_ap, tb[:, 0:1024].rearrange("p (c n) -> p c n", n=128),
                rd=[ppg(f"T{half % 2}")], wr=[wrpg])

    dma("sync", ident_b.ap(), d_ident, [], [ident_b.pg()])
    vop("memset", c_eps.ap(), EPS, wr=[c_eps.pg()])
    vop("memset", c_one.ap(), 1.0, wr=[c_one.pg()])
    vop("memset", c_qs.ap(), QSCALE, wr=[c_qs.pg()])

    SCALE_UNITS = []

    def scale_cols(src_d, n_elem, dst):
        ncols = n_elem // 128
        done = 0
        while done < ncols:
            take = min(2, ncols - done)
            def unit(done=done, take=take, src_d=src_d, dst=dst):
                dma("sync", sc_row.row0(0, take * 128), src_d[:, done * 128:(done + take) * 128],
                    [], [sc_row.pg(0, 256)])
                srcs = [(sc_row.row0(c * 128, (c + 1) * 128), [sc_row.pg(0, 256), c_one.pg()])
                        for c in range(take)]
                tr_batch(srcs, "A0", lambda i: pA[0][:, i:i + 1], c_one.row0(0, 1))
                vop("tensor_copy", dst.ap(done, done + take), pA[0][:, 0:take],
                    rd=[ppg("A0")], wr=[dst.pg(done, done + take)])
            SCALE_UNITS.append(unit)
            done += take
    scale_cols(d_gs, INTER, gs_col)
    scale_cols(d_us, INTER, us_col)
    scale_cols(d_ds, D, ds_col)

    sB = slotB
    w_stg = [Region(h_bf, "bf", sB.off + k * 2 * D, 2 * D, f"wstg{k}", BF_TOTAL) for k in range(2)]
    wT_stg = [Region(h_bf, "bf", sB.off + 4 * D + k * D, D, f"wTstg{k}", BF_TOTAL) for k in range(2)]
    dw_stg = [Region(h_bf, "bf", sB.off + k * 2816, 2816, f"dwstg{k}", BF_TOTAL) for k in range(2)]
    dwT_stg = [Region(h_bf, "bf", sB.off + 5632 + k * 2816, 2816, f"dwTstg{k}", BF_TOTAL) for k in range(2)]

    PHW_UNITS = []

    def _phw_gu_unit(src_d, gu, i2):
        def unit():
            stg = w_stg[i2 % 2]
            dma("gpsimd", stg.ap().rearrange("p (s h) -> p s h", s=2),
                src_d.rearrange("(s2 s p) h -> p s2 s h", s=2, p=128)[:, i2],
                [], [stg.pg()])
            for s in range(2):
                i = i2 * 2 + s
                tstg = wT_stg[i % 2]
                for half in range(2):
                    par = (i * 2 + half) % 2
                    tb = pT[par]
                    srcs = [(stg.ap(s * D + (half * 8 + c) * 128, s * D + (half * 8 + c + 1) * 128),
                             [stg.pg(s * D + (half * 8 + c) * 128, s * D + (half * 8 + c + 1) * 128)])
                            for c in range(8)]
                    tr_batch(srcs, f"T{par}", lambda i_, tb=tb: tb[:, i_ * 128:(i_ + 1) * 128], ident_b.ap())
                    vop("tensor_copy", tstg.ap(half * 1024, half * 1024 + 1024), tb[:, 0:1024],
                        rd=[ppg(f"T{par}")], wr=[tstg.pg(half * 1024, half * 1024 + 1024)],
                        eng="vector")
                dma("sync", d_guwT[i, gu], tstg.ap(), [tstg.pg()], [[("dram_guw", i * 2 + gu)]])
        return unit

    def _phw_dw_unit(hc, half):
        def unit():
            stg = dw_stg[(hc * 2 + half) % 2]
            dma("gpsimd", stg.ap(),
                d_dw.rearrange("(hc p) i -> hc p i", p=128)[hc][:, half * 2816:(half + 1) * 2816],
                [], [stg.pg()])
            tstg = dwT_stg[(hc * 2 + half) % 2]
            for b2 in range(3):
                hn = min(8, 22 - b2 * 8)
                par = b2 % 2
                tb = pT[par]
                srcs = [(stg.ap((b2 * 8 + c) * 128, (b2 * 8 + c + 1) * 128),
                         [stg.pg((b2 * 8 + c) * 128, (b2 * 8 + c + 1) * 128)])
                        for c in range(hn)]
                tr_batch(srcs, f"T{par}", lambda i_, tb=tb: tb[:, i_ * 128:(i_ + 1) * 128], ident_b.ap())
                vop("tensor_copy", tstg.ap(b2 * 1024, b2 * 1024 + hn * 128), tb[:, 0:hn * 128],
                    rd=[ppg(f"T{par}")], wr=[tstg.pg(b2 * 1024, b2 * 1024 + hn * 128)])
            dma("sync", d_dwT[hc][:, half * 2816:(half + 1) * 2816], tstg.ap(), [tstg.pg()], [[("dram_dw", hc * 2 + half)]])
        return unit

    PHW_UNITS.extend(SCALE_UNITS)
    for i2_ in range(NICH // 2):
        PHW_UNITS.append(_phw_gu_unit(d_gw, 0, i2_))
    for i2_ in range(NICH // 2):
        PHW_UNITS.append(_phw_gu_unit(d_uw, 1, i2_))
    for hc_ in range(NCH):
        for half_ in range(2):
            PHW_UNITS.append(_phw_dw_unit(hc_, half_))

    def pump(k=1):
        for _ in range(k):
            if PHW_UNITS:
                PHW_UNITS.pop(0)()

    def ph_load_x(g, R):
        for t in range(8):
            dma("gpsimd", R.ap(t * D, (t + 1) * D),
                d_x[(g * GT + t * 128):(g * GT + (t + 1) * 128), :],
                [], [R.pg(t * D, (t + 1) * D)])

    def ph_ln1(R, ntile, tokspan):
        for t in range(ntile):
            ln_stats_pass1(R, t * D, (t + 1) * D, t, dv=dumpA_v, ds_=dumpA_s)
        ln_stats_pass2(ntile)
        for t in range(ntile):
            xs = xnst[t % 2]
            ln_normalize(R, t * D, (t + 1) * D, t, xs)
            transpose_tile(xs, featA, tokspan, t * 128)
            pump(1)

    def kv_proj(keyoff, ttiles, tokspan):
        for m in range(4):
            for t in range(ttiles // 4):
                par = (m * 2 + t) % 2
                items = []
                for c in range(NCH):
                    items.append((wkb.ap(c * KVD + m * 128, c * KVD + (m + 1) * 128),
                                  featA.ap(c * tokspan + t * 512, c * tokspan + t * 512 + 512),
                                  pA[par][:, 0:512],
                                  [wkb.pg(c * KVD + m * 128, c * KVD + (m + 1) * 128),
                                   featA.pg(c * tokspan + t * 512, c * tokspan + t * 512 + 512)]))
                mm_group(items, wr=[ppg(f"A{par}")])
                dst_a = m * 1536 + keyoff + t * 512
                sevac(kT.ap(dst_a, dst_a + 512), pA[par][:, 0:512],
                      [ppg(f"A{par}")], [kT.pg(dst_a, dst_a + 512)])
                pump(1)
        for t8 in range(ttiles):
            par = t8 % 2
            items = []
            for c in range(NCH):
                items.append((featA.ap(c * tokspan + t8 * 128, c * tokspan + (t8 + 1) * 128),
                              wvb.ap(c * KVD, (c + 1) * KVD),
                              pS[par][:, 0:512],
                              [featA.pg(c * tokspan + t8 * 128, c * tokspan + (t8 + 1) * 128),
                               wvb.pg(c * KVD, (c + 1) * KVD)]))
            mm_group(items, wr=[ppg(f"S{par}a")])
            koff = keyoff + t8 * 128
            sevac(Vl.ap(koff * 4, koff * 4 + 512), pS[par][:, 0:512],
                  [ppg(f"S{par}a")], [Vl.pg(koff * 4, koff * 4 + 512)])
            pump(1)

    def ph_qkv(g):
        if g > 0:
            dma("gpsimd", wkb.ap().rearrange("p (c m) -> p c m", c=NCH),
                d_wk.rearrange("(c p) m -> p c m", p=128), [], [wkb.pg()])
            dma("gpsimd", wvb.ap().rearrange("p (c m) -> p c m", c=NCH),
                d_wv.rearrange("(c p) m -> p c m", p=128), [], [wvb.pg()])
        kv_proj(512, 8, GT)
        for ms in range(16):
            wq = wqb[ms % 2]
            dma("gpsimd", wq.ap().rearrange("p (c m) -> p c m", c=NCH),
                d_wq.rearrange("(c p) m -> p c m", p=128)[:, :, ms * 128:(ms + 1) * 128],
                [], [wq.pg()])
            for t in range(2):
                par = (ms * 2 + t) % 2
                items = []
                for c in range(NCH):
                    items.append((wq.ap(c * 128, (c + 1) * 128),
                                  featA.ap(c * GT + t * 512, c * GT + t * 512 + 512),
                                  pA[par][:, 0:512],
                                  [wq.pg(c * 128, (c + 1) * 128),
                                   featA.pg(c * GT + t * 512, c * GT + t * 512 + 512)]))
                mm_group(items, wr=[ppg(f"A{par}")])
                sevac(qT.ap(ms * GT + t * 512, ms * GT + t * 512 + 512), pA[par][:, 0:512],
                      [ppg(f"A{par}"), c_qs.pg()],
                      [qT.pg(ms * GT + t * 512, ms * GT + t * 512 + 512)],
                      scale=c_qs.ap(0, 1))
                pump(1)

    def ph_attn(g):
        dma("sync", masks.ap(), d_mask[:, g * 8 * 640:(g + 1) * 8 * 640], [], [masks.pg()])
        iters = [(qt, h) for qt in range(8) for h in range(H)]
        n = len(iters)

        def st_S(j):
            qt, h = iters[j]
            par = j % 2
            hkv = h // 4
            Sp = pS[par]
            lhs = qT.ap(h * GT + qt * 128, h * GT + (qt + 1) * 128)
            lpg = qT.pg(h * GT + qt * 128, h * GT + (qt + 1) * 128)
            kbase = hkv * 1536 + qt * 128
            items = [(lhs, kT.ap(kbase, kbase + 512), Sp[:, 0:512],
                      [lpg, kT.pg(kbase, kbase + 512)]),
                     (lhs, kT.ap(kbase + 512, kbase + 640), Sp[:, 512:640],
                      [kT.pg(kbase + 512, kbase + 640)])]
            allrd = list(items[0][3]) + list(items[1][3])
            for i, (l, r, o, rdp) in enumerate(items):
                def b(e, cb, l=l, r=r, o=o):
                    cb(e.matmul(o, l, r, start=True, stop=True))
                Spg = [("psum", f"S{par}a"), ("psum", f"S{par}b")]
                bld.op("tensor", b, reads=rdp,
                       writes=[Spg] if i == 1 else (),
                       wwaits=[Spg] if i == 0 else (),
                       sem="tS", inc=1 if i == 1 else 0,
                       record_reads=allrd if i == 1 else ())

        def st_max(j):
            par = j % 2
            Spg = [("psum", f"S{par}a"), ("psum", f"S{par}b")]
            vop("reduce_max", nmax.ap(par, par + 1), pS[par][:, 0:640], axis=AX.X,
                negate=True, rd=[Spg], wr=[nmax.pg(par, par + 1)])

        def st_exp(j):
            par = j % 2
            Spg = [("psum", f"S{par}a"), ("psum", f"S{par}b")]
            vop("activation", p_st[par].ap(), pS[par][:, 0:640], AF.Exp,
                bias=nmax.ap(par, par + 1), eng="scalar",
                rd=[Spg, nmax.pg(par, par + 1)], wr=[p_st[par].pg()])

        def st_msk(j):
            qt, h = iters[j]
            par = j % 2
            vop("scalar_tensor_tensor", p_m[par].ap(), p_st[par].ap(), 1.0,
                masks.ap(qt * 640, (qt + 1) * 640), ALU.mult, ALU.mult,
                accum_out=rsum.ap(par, par + 1),
                rd=[p_st[par].pg(), masks.pg(qt * 640, (qt + 1) * 640)],
                wr=[p_m[par].pg(), rsum.pg(par, par + 1)])
            vop("reciprocal", rcp.ap(par, par + 1), rsum.ap(par, par + 1),
                rd=[rsum.pg(par, par + 1)], wr=[rcp.pg(par, par + 1)])
            vop("tensor_scalar_mul", p_n[par].ap(), p_m[par].ap(), rcp.ap(par, par + 1),
                rd=[p_m[par].pg(), rcp.pg(par, par + 1)], wr=[p_n[par].pg()])

        def st_pt(j):
            par = j % 2
            srcs = [(p_n[par].ap(c * 128, (c + 1) * 128), [p_n[par].pg(c * 128, (c + 1) * 128)])
                    for c in range(5)]
            tr_batch(srcs, f"T{par}", lambda i, tb=pT[par]: tb[:, i * 128:(i + 1) * 128],
                     ident_b.ap())
            sevac(ptk[par].ap(), pT[par][:, 0:640], [ppg(f"T{par}")], [ptk[par].pg()])

        def st_pv(j):
            qt, h = iters[j]
            par = j % 2
            hkv = h // 4
            items = []
            for c in range(5):
                vofs = (qt + c) * 512 + hkv * 128
                items.append((Vl.ap(vofs, vofs + 128),
                              ptk[par].ap(c * 128, (c + 1) * 128),
                              pA[par][:, 0:128],
                              [Vl.pg(vofs, vofs + 128), ptk[par].pg(c * 128, (c + 1) * 128)]))
            mm_group(items, wr=[ppg(f"A{par}")])
            sevac(featA.ap(h * GT + qt * 128, h * GT + (qt + 1) * 128),
                  pA[par][:, 0:128], [ppg(f"A{par}")],
                  [featA.pg(h * GT + qt * 128, h * GT + (qt + 1) * 128)])

        for j in range(n + 5):
            pump(1)
            if 0 <= j - 5 < n:
                st_pv(j - 5)
            if 0 <= j - 4 < n:
                st_pt(j - 4)
            if 0 <= j - 3 < n:
                st_msk(j - 3)
            if 0 <= j - 2 < n:
                st_exp(j - 2)
            if 0 <= j - 1 < n:
                st_max(j - 1)
            if j < n:
                st_S(j)

    def ph_wo_ln2(g, R, R2):
        LB = 8
        for hs in range(16):
            wo = wob[hs % 2]
            dma("gpsimd", wo.ap().rearrange("p (c m) -> p c m", c=NCH),
                d_wo.rearrange("(c p) m -> p c m", p=128)[:, :, hs * 128:(hs + 1) * 128],
                [], [wo.pg()])
            for t in range(2):
                par = (hs * 2 + t) % 2
                items = []
                for c in range(NCH):
                    items.append((wo.ap(c * 128, (c + 1) * 128),
                                  featA.ap(c * GT + t * 512, c * GT + t * 512 + 512),
                                  pA[par][:, 0:512],
                                  [wo.pg(c * 128, (c + 1) * 128),
                                   featA.pg(c * GT + t * 512, c * GT + t * 512 + 512)]))
                mm_group(items, wr=[ppg(f"A{par}")])
                yt = ytr[par]
                vop("tensor_copy", yt.ap(), pA[par][:, 0:512],
                    rd=[ppg(f"A{par}")], wr=[yt.pg()])
                srcs = [(yt.ap(i * 128, (i + 1) * 128), [yt.pg(i * 128, (i + 1) * 128)])
                        for i in range(4)]
                tr_batch(srcs, f"T{par}", lambda i, tb=pT[par]: tb[:, i * 128:(i + 1) * 128],
                         ident_b.ap())
                dst = ybuf.sap(t * 4 * D + hs * 128, [[D, 4], [1, 128]])
                wrpg = []
                for i in range(4):
                    wrpg += ybuf.pg((t * 4 + i) * D + hs * 128, (t * 4 + i) * D + (hs + 1) * 128)
                sevac(dst, pT[par][:, 0:512].rearrange("p (c n) -> p c n", n=128),
                      [ppg(f"T{par}")], [wrpg])
        for t in range(8):
            ln_stats_pass1(ybuf, t * D, (t + 1) * D, LB + t, dv=dumpB_v, ds_=dumpB_s)
        ln_stats_pass2(8, base=LB)
        for t in range(8):
            yn = ynst[t % 2]
            ln_normalize(ybuf, t * D, (t + 1) * D, LB + t, yn)
            vop("tensor_add", R2.ap(t * D, (t + 1) * D), R.ap(t * D, (t + 1) * D), yn.ap(),
                eng="gpsimd",
                rd=[R.pg(t * D, (t + 1) * D), yn.pg()], wr=[R2.pg(t * D, (t + 1) * D)])

    fw2 = [Region(h_bf, "bf", fw_stg[0].off, 2 * D, "fw2_0", BF_TOTAL),
           Region(h_bf, "bf", fw_stg[2].off, 2 * D, "fw2_1", BF_TOTAL),
           Region(h_bf, "bf", wqb[0].off, 2 * D, "fw2_2", BF_TOTAL)]

    def ph_ffn(g, R2, dump=False, d_dbg=None, g_=0):
        y2off = NCH * 512

        def ln3(sgi):
            for t4 in range(4):
                t = sgi * 4 + t4
                ln_stats_pass1(R2, t * D, (t + 1) * D, 16 + t4, dv=dumpC_v, ds_=dumpC_s)
            ln_stats_pass2(4, base=16)
            for t4 in range(4):
                t = sgi * 4 + t4
                xs = xnst[t4 % 2]
                ln_normalize(R2, t * D, (t + 1) * D, 16 + t4, xs)
                transpose_tile(xs, featA, 512, t4 * 128)

        def gu(sgi):
            for i in range(NICH):
                par = i % 2
                fw = fw2[i % 3]
                dma("sync", fw.ap(0, 2 * D).rearrange("p (s h) -> p s h", s=2),
                    d_guwT[i].rearrange("s p h -> p s h"),
                    [[("dram_guw", i * 2), ("dram_guw", i * 2 + 1)]], [fw.pg()])
                ig = []
                iu = []
                for c in range(NCH):
                    ig.append((fw.ap(c * 128, (c + 1) * 128),
                               featA.ap(c * 512, c * 512 + 512), pA[par][:, 0:512],
                               [fw.pg(c * 128, (c + 1) * 128), featA.pg(c * 512, c * 512 + 512)]))
                    iu.append((fw.ap(D + c * 128, D + (c + 1) * 128),
                               featA.ap(c * 512, c * 512 + 512), pS[par][:, 0:512],
                               [fw.pg(D + c * 128, D + (c + 1) * 128), featA.pg(c * 512, c * 512 + 512)]))
                mm_group(ig, wr=[ppg(f"A{par}")])
                mm_group(iu, wr=[ppg(f"S{par}a")])
                sgs = sgsb[par]
                vop("activation", sgs.ap(), pA[par][:, 0:512], AF.Silu, scale=gs_col.ap(i, i + 1),
                    eng="scalar", rd=[ppg(f"A{par}"), gs_col.pg(i, i + 1)], wr=[sgs.pg()])
                vop("scalar_tensor_tensor", hdnT.ap(i * 512, (i + 1) * 512), pS[par][:, 0:512],
                    us_col.ap(i, i + 1), sgs.ap(), ALU.mult, ALU.mult,
                    rd=[ppg(f"S{par}a"), us_col.pg(i, i + 1), sgs.pg()],
                    wr=[hdnT.pg(i * 512, (i + 1) * 512)])

        def down(sgi):
            for hc in range(NCH):
                par = hc % 2
                dwl = dwl_stg[par]
                dma("sync", dwl.ap(), d_dwT[hc], [[("dram_dw", hc * 2), ("dram_dw", hc * 2 + 1)]], [dwl.pg()])
                items = []
                for c in range(NICH):
                    items.append((dwl.ap(c * 128, (c + 1) * 128),
                                  hdnT.ap(c * 512, (c + 1) * 512),
                                  pA[par][:, 0:512],
                                  [dwl.pg(c * 128, (c + 1) * 128), hdnT.pg(c * 512, (c + 1) * 512)]))
                mm_group(items, wr=[ppg(f"A{par}")])
                yt = ytr[par]
                sevac(yt.ap(), pA[par][:, 0:512],
                      [ppg(f"A{par}"), ds_col.pg(hc, hc + 1)], [yt.pg()],
                      scale=ds_col.ap(hc, hc + 1))
                srcs = [(yt.ap(i * 128, (i + 1) * 128), [yt.pg(i * 128, (i + 1) * 128)])
                        for i in range(4)]
                tr_batch(srcs, f"T{par}", lambda i, tb=pT[par]: tb[:, i * 128:(i + 1) * 128],
                         ident_b.ap())
                dst = featA.sap(y2off + hc * 128, [[D, 4], [1, 128]])
                wrpg = []
                for i in range(4):
                    wrpg += featA.pg(y2off + i * D + hc * 128, y2off + i * D + (hc + 1) * 128)
                vop("tensor_copy", dst, pT[par][:, 0:512].rearrange("p (c n) -> p c n", n=128),
                    rd=[ppg(f"T{par}")], wr=[wrpg])
            if dump and sgi == 0:
                dma("sync", d_dbg[f"hdnT{g_}"], hdnT.ap(), [hdnT.pg()], [])
                dma("sync", d_dbg[f"y2{g_}"], featA.ap(y2off, y2off + 4 * D),
                    [featA.pg(y2off, y2off + 4 * D)], [])

        def epi(sgi):
            for t4 in range(4):
                ln_stats_pass1(featA, y2off + t4 * D, y2off + (t4 + 1) * D, 20 + t4,
                               dv=dump4_v, ds_=dump4_s)
            ln_stats_pass2(4, base=20)
            for t4 in range(4):
                t = sgi * 4 + t4
                yn = ynst[t4 % 2]
                ln_normalize(featA, y2off + t4 * D, y2off + (t4 + 1) * D, 20 + t4, yn)
                ost = out_stg[t4 % 2]
                vop("tensor_add", ost.ap(), R2.ap(t * D, (t + 1) * D), yn.ap(),
                    eng="gpsimd",
                    rd=[R2.pg(t * D, (t + 1) * D), yn.pg()], wr=[ost.pg()])
                dma("gpsimd", d_out[(g * GT + t * 128):(g * GT + (t + 1) * 128), :], ost.ap(),
                    [ost.pg()], [])

        ln3(0)
        gu(0)
        down(0)
        ln3(1)
        epi(0)
        gu(1)
        down(1)
        epi(1)

    dma("gpsimd", wkb.ap().rearrange("p (c m) -> p c m", c=NCH),
        d_wk.rearrange("(c p) m -> p c m", p=128), [], [wkb.pg()])
    dma("gpsimd", wvb.ap().rearrange("p (c m) -> p c m", c=NCH),
        d_wv.rearrange("(c p) m -> p c m", p=128), [], [wvb.pg()])
    for t in range(4):
        dma("gpsimd", slotA.ap(t * D, (t + 1) * D), d_xh[t * 128:(t + 1) * 128, :],
            [], [slotA.pg(t * D, (t + 1) * D)])
    ph_ln1(slotA, 4, WIN)
    kv_proj(0, 4, WIN)

    for g in range(NG):
        R, R2 = slotA, slotB
        if g > 0:
            for m in range(4):
                vop("tensor_copy", kT.ap(m * 1536, m * 1536 + 512),
                    kT.ap(m * 1536 + 1024, m * 1536 + 1536),
                    rd=[kT.pg(m * 1536 + 1024, m * 1536 + 1536)],
                    wr=[kT.pg(m * 1536, m * 1536 + 512)])
            vop("tensor_copy", Vl.ap(0, 2048), Vl.ap(4096, 6144),
                rd=[Vl.pg(4096, 6144)], wr=[Vl.pg(0, 2048)])
        if g == 0:
            ph_load_x(g, R)
        ph_ln1(R, 8, GT)
        if DEBUG:
            dma("sync", d_dbg[f"xnT{g}"], featA.ap(), [featA.pg()], [])
        ph_qkv(g)
        if DEBUG:
            dma("sync", d_dbg[f"kT{g}"], kT.ap(), [kT.pg()], [])
            dma("sync", d_dbg[f"Vl{g}"], Vl.ap(), [Vl.pg()], [])
            dma("sync", d_dbg[f"qT{g}"], qT.ap(), [qT.pg()], [])
        pump(len(PHW_UNITS))
        ph_attn(g)
        pump(len(PHW_UNITS))
        if DEBUG:
            dma("sync", d_dbg[f"att{g}"], featA.ap(), [featA.pg()], [])
        ph_wo_ln2(g, R, R2)
        if DEBUG:
            dma("sync", d_dbg[f"ybuf{g}"], ybuf.ap(), [ybuf.pg()], [])
            dma("sync", d_dbg[f"R2{g}"], R2.ap(), [R2.pg()], [])
        if g + 1 < NG:
            ph_load_x(g + 1, slotA)
        ph_ffn(g, R2, dump=DEBUG, d_dbg=d_dbg, g_=g)

    final_waits = {s: v for s, v in bld.semcnt.items()
                   if s.startswith("dG") or s.startswith("dS")}
    def runfin(e, semmap):
        for s, v in sorted(final_waits.items()):
            e.wait_ge(semmap[s], v)
    bld.streams["sync"].append(runfin)

    with st:
        semctx = contextlib.ExitStack()
        semmap = {}
        names = set(bld.semcnt) | {"tS", "vS", "sS"}
        for name in sorted(names):
            semmap[name] = semctx.enter_context(nc.semaphore(name))
        with semctx:
            with nc.Block() as block:
                @block.sync
                def _(e):
                    for f in bld.streams["sync"]:
                        f(e, semmap)
                @block.gpsimd
                def _(e):
                    for f in bld.streams["gpsimd"]:
                        f(e, semmap)
                @block.tensor
                def _(e):
                    for f in bld.streams["tensor"]:
                        f(e, semmap)
                @block.vector
                def _(e):
                    for f in bld.streams["vector"]:
                        f(e, semmap)
                @block.scalar
                def _(e):
                    for f in bld.streams["scalar"]:
                        f(e, semmap)
    return nc


def _build_masks():
    out = []
    qq = np.arange(128)[:, None]
    s = np.arange(640)[None, :]
    base = (s > qq) & (s <= 512 + qq)
    for c in range(NCORE):
        seq_off = (c % 4) * TOK
        m = np.zeros((128, 16 * 640), dtype=np.float32)
        for qt in range(16):
            key_seq = seq_off + qt * 128 + s - 512
            ok = base & (key_seq >= 0)
            m[:, qt * 640:(qt + 1) * 640] = ok
        out.append(m.astype(BF))
    return out


_cache = {}


def kernel(L0_attn_x, L0_attn_Wq, L0_attn_Wk, L0_attn_Wv, L0_attn_Wo,
           L0_ffn_gate_w, L0_ffn_up_w, L0_ffn_down_w,
           L0_ffn_gate_s, L0_ffn_up_s, L0_ffn_down_s):
    x = np.ascontiguousarray(np.asarray(L0_attn_x, dtype=np.float32))
    if "nc" not in _cache:
        _cache["nc"] = build_program()
        _cache["masks"] = _build_masks()
        _cache["ident"] = np.eye(128).astype(BF)
    nc = _cache["nc"]
    masks = _cache["masks"]
    ident = _cache["ident"]

    Wq = np.ascontiguousarray(np.asarray(L0_attn_Wq, np.float32))
    Wk = np.ascontiguousarray(np.asarray(L0_attn_Wk, np.float32))
    Wv = np.ascontiguousarray(np.asarray(L0_attn_Wv, np.float32))
    Wo = np.ascontiguousarray(np.asarray(L0_attn_Wo, np.float32))
    gw = np.ascontiguousarray(np.asarray(L0_ffn_gate_w, np.float32))
    uw = np.ascontiguousarray(np.asarray(L0_ffn_up_w, np.float32))
    dw = np.ascontiguousarray(np.asarray(L0_ffn_down_w, np.float32))
    gs = np.ascontiguousarray(np.asarray(L0_ffn_gate_s, np.float32)).reshape(1, -1)
    us = np.ascontiguousarray(np.asarray(L0_ffn_up_s, np.float32)).reshape(1, -1)
    ds = np.ascontiguousarray(np.asarray(L0_ffn_down_s, np.float32)).reshape(1, -1)

    in_maps = []
    for c in range(NCORE):
        b, w = c // 4, c % 4
        xs = np.ascontiguousarray(x[b, w * TOK:(w + 1) * TOK, :])
        if w == 0:
            xh = np.zeros((WIN, D), np.float32)
        else:
            xh = np.ascontiguousarray(x[b, w * TOK - WIN:w * TOK, :])
        in_maps.append({
            "x": xs, "xh": xh, "mask": masks[c], "ident": ident,
            "Wq": Wq, "Wk": Wk, "Wv": Wv, "Wo": Wo,
            "gw": gw, "uw": uw, "dw": dw, "gs": gs, "us": us, "ds": ds,
        })
    res = run_bass_kernel_spmd(nc, in_maps, core_ids=list(range(NCORE)))
    out = np.empty((B, T, D), np.float32)
    for c in range(NCORE):
        b, w = c // 4, c % 4
        out[b, w * TOK:(w + 1) * TOK, :] = res.results[c]["out"]
    if DEBUG:
        _cache["dbg"] = res.results
    return out


# version 15
# speedup vs baseline: 1.2715x; 1.2715x over previous
import contextlib
import numpy as np
import ml_dtypes

import concourse.bass as bass
import concourse.mybir as mybir
from concourse.bass_utils import run_bass_kernel_spmd

F32 = mybir.dt.float32
BF16 = mybir.dt.bfloat16
AF = mybir.ActivationFunctionType
ALU = mybir.AluOpType
AX = mybir.AxisListType
BF = ml_dtypes.bfloat16

B, T, D = 2, 8192, 2048
NCORE = 8
TOK = 2048
WIN = 512
HK, NREP, HD = 4, 4, 128
H = HK * NREP
KVD = HK * HD
INTER = 5632
NCH = D // 128
NICH = INTER // 128
GT = 1024
NG = TOK // GT
SG = 512
NSG = GT // SG
QSCALE = float(1.0 / np.sqrt(HD))
EPS = 1e-5

ENGS = ("sync", "gpsimd", "tensor", "vector", "scalar")
SPACING = 6
PAGE = 128


class Tok:
    __slots__ = ("eng", "idx", "sem", "val", "is_dma")
    def __init__(self, eng, idx, sem, val, is_dma=False):
        self.eng, self.idx, self.sem, self.val, self.is_dma = eng, idx, sem, val, is_dma


class Page:
    __slots__ = ("w", "r")
    def __init__(self):
        self.w = None
        self.r = []


class Builder:
    def __init__(self):
        self.streams = {e: [] for e in ENGS}
        self.icount = {e: 0 for e in ENGS}
        self.semcnt = {}
        self.maxseen = {e: {} for e in ENGS}
        self.pages = {}

    def sem_next(self, name, n):
        v = self.semcnt.get(name, 0) + n
        self.semcnt[name] = v
        return v

    def page(self, key):
        p = self.pages.get(key)
        if p is None:
            p = Page()
            self.pages[key] = p
        return p

    def _want(self, eng, tok, waits):
        if tok is None:
            return
        if tok.eng == eng and not tok.is_dma:
            if self.icount[eng] - tok.idx >= SPACING:
                return
        if self.maxseen[eng].get(tok.sem, 0) >= tok.val:
            return
        if tok.val > waits.get(tok.sem, 0):
            waits[tok.sem] = tok.val

    def op(self, eng, build, reads=(), writes=(), wwaits=(), sem=None, inc=0,
           is_dma=False, record_reads=(), extra_waits=()):
        waits = {}
        for pk in reads:
            for k in pk:
                self._want(eng, self.page(k).w, waits)
        for pk in list(writes) + list(wwaits):
            for k in pk:
                p = self.page(k)
                self._want(eng, p.w, waits)
                for t in p.r:
                    self._want(eng, t, waits)
        for s, v in extra_waits:
            if self.maxseen[eng].get(s, 0) < v and v > waits.get(s, 0):
                waits[s] = v
        if writes:
            assert inc > 0, "writing op must carry a semaphore increment"

        myidx = self.icount[eng]
        tok = None
        if inc:
            val = self.sem_next(sem, inc)
            tok = Tok(eng, myidx, sem, val, is_dma=is_dma)

        wlist = sorted(waits.items())
        for s, v in wlist:
            if self.maxseen[eng].get(s, 0) < v:
                self.maxseen[eng][s] = v
        semname = sem

        def run(e, semmap):
            for s, v in wlist:
                e.wait_ge(semmap[s], v)
            holder = []
            build(e, holder.append)
            assert len(holder) == 1
            if tok is not None:
                holder[0].then_inc(semmap[semname], inc)

        self.streams[eng].append(run)
        self.icount[eng] += 1
        for pk in writes:
            for k in pk:
                p = self.page(k)
                p.w = tok
                p.r = []
        if tok is not None:
            for pk in list(reads) + list(record_reads):
                for k in pk:
                    self.page(k).r.append(tok)
        return tok


class Region:
    __slots__ = ("handle", "off", "n", "space", "name", "ncol")
    def __init__(self, handle, space, off, n, name, ncol):
        self.handle, self.space, self.off, self.n = handle, space, off, n
        self.name, self.ncol = name, ncol

    def ap(self, a=0, b=None):
        if b is None:
            b = self.n
        return self.handle[:, self.off + a:self.off + b]

    def row0(self, a=0, b=None):
        if b is None:
            b = self.n
        return self.handle[0:1, self.off + a:self.off + b]

    def sap(self, elem_off, dims):
        return bass.AP(self.handle, self.off + elem_off, [[self.ncol, 128]] + dims)

    def pg(self, a=0, b=None):
        if b is None:
            b = self.n
        lo = (self.off + a) // PAGE
        hi = (self.off + b - 1) // PAGE
        return [(self.space, i) for i in range(lo, hi + 1)]


class Alloc:
    def __init__(self, handle, space, total, round_to=PAGE):
        self.handle, self.space, self.total = handle, space, total
        self.round_to = round_to
        self.off = 0
        self.marks = []
        self.peak = 0

    def take(self, n, name):
        r_ = self.round_to
        n = ((n + r_ - 1) // r_) * r_
        r = Region(self.handle, self.space, self.off, n, name, self.total)
        self.off += n
        self.peak = max(self.peak, self.off)
        assert self.off <= self.total, f"{self.space} overflow at {name}: {self.off} > {self.total}"
        return r

    def mark(self):
        self.marks.append(self.off)

    def reset(self):
        self.off = self.marks.pop()


DEBUG = False

def build_program():
    nc = bass.Bass("TRN2", target_bir_lowering=False)

    d_x = nc.dram_tensor("x", [TOK, D], F32, kind="ExternalInput").ap()
    d_xh = nc.dram_tensor("xh", [WIN, D], F32, kind="ExternalInput").ap()
    d_mask = nc.dram_tensor("mask", [128, 16 * 640], BF16, kind="ExternalInput").ap()
    d_ident = nc.dram_tensor("ident", [128, 128], BF16, kind="ExternalInput").ap()
    d_wq = nc.dram_tensor("Wq", [D, D], F32, kind="ExternalInput").ap()
    d_wk = nc.dram_tensor("Wk", [D, KVD], F32, kind="ExternalInput").ap()
    d_wv = nc.dram_tensor("Wv", [D, KVD], F32, kind="ExternalInput").ap()
    d_wo = nc.dram_tensor("Wo", [D, D], F32, kind="ExternalInput").ap()
    d_gw = nc.dram_tensor("gw", [INTER, D], F32, kind="ExternalInput").ap()
    d_uw = nc.dram_tensor("uw", [INTER, D], F32, kind="ExternalInput").ap()
    d_dw = nc.dram_tensor("dw", [D, INTER], F32, kind="ExternalInput").ap()
    d_gs = nc.dram_tensor("gs", [1, INTER], F32, kind="ExternalInput").ap()
    d_us = nc.dram_tensor("us", [1, INTER], F32, kind="ExternalInput").ap()
    d_ds = nc.dram_tensor("ds", [1, D], F32, kind="ExternalInput").ap()
    d_out = nc.dram_tensor("out", [TOK, D], F32, kind="ExternalOutput").ap()
    d_dbg = {}
    if DEBUG:
        for g_ in range(2):
            for nm, n in [("xnT", NCH * GT), ("kT", 4 * 1536), ("Vl", 12 * KVD),
                          ("qT", H * GT), ("att", NCH * GT), ("ybuf", 8 * D),
                          ("R2", 8 * D), ("hdnT", NICH * SG), ("y2", 4 * D)]:
                d_dbg[f"{nm}{g_}"] = nc.dram_tensor(f"dbg_{nm}{g_}", [128, n], BF16, kind="ExternalOutput").ap()
    d_guwT = nc.dram_tensor("guwT_s", [NICH, 2, 128, D], BF16).ap()
    d_dwT = nc.dram_tensor("dwT_s", [NCH, 128, NICH * 128], BF16).ap()

    bld = Builder()
    st = contextlib.ExitStack()

    BF_TOTAL = 95360
    F32_TOTAL = 4992
    h_bf = st.enter_context(nc.sbuf_tensor("abf", [128, BF_TOTAL], BF16))
    h_f = st.enter_context(nc.sbuf_tensor("af32", [128, F32_TOTAL], F32))
    A = Alloc(h_bf, "bf", BF_TOTAL)
    Fa = Alloc(h_f, "f32", F32_TOTAL, round_to=32)

    slotA = A.take(8 * D, "slotA")
    slotB = A.take(8 * D, "slotB")
    featA = A.take(NCH * GT, "featA")
    kT = A.take(4 * 1536, "kT")
    Vl = A.take(12 * KVD, "Vl")
    ident_b = A.take(128, "ident")

    A.mark()
    qT = A.take(H * GT, "qT")
    wkb = A.take(NCH * KVD, "wkb")
    wvb = A.take(NCH * KVD, "wvb")
    A.reset(); A.mark()
    hdnT = A.take(NICH * SG, "hdnT")
    fw_stg = [A.take(D, f"fwstg{i}") for i in range(4)]
    A.reset(); A.mark()
    _p1 = A.take(NICH * SG, "_p1")
    dwl_stg = [A.take(NICH * 128, f"dwl{i}") for i in range(2)]
    A.reset(); A.mark()
    _p2 = A.take(H * GT, "_p2")
    masks = A.take(8 * 640, "masks")
    p_st = [A.take(640, f"p_st{i}") for i in range(2)]
    p_m = [A.take(640, f"p_m{i}") for i in range(2)]
    p_n = [A.take(640, f"p_n{i}") for i in range(2)]
    ptk = [A.take(640, f"ptk{i}") for i in range(2)]
    wob = [A.take(NCH * 128, f"wob{i}") for i in range(2)]
    ytr = [A.take(512, f"ytr{i}") for i in range(2)]
    sgsb = [A.take(512, f"sgsb{i}") for i in range(2)]
    A.reset(); A.mark()
    _p2b = A.take(H * GT, "_p2b")
    ynst = [A.take(D, f"ynst{i}") for i in range(2)]
    A.reset(); A.mark()
    dump4_v = A.take(D, "dump4_v")
    dump4_s = A.take(D, "dump4_s")
    A.reset(); A.mark()
    ybuf = A.take(8 * D, "ybuf")
    A.reset(); A.mark()
    w_stg = [A.take(2 * D, f"wstg{i}") for i in range(2)]
    wT_stg = [A.take(D, f"wTstg{i}") for i in range(2)]
    A.reset(); A.mark()
    dw_stg = [A.take(INTER, f"dwstg{i}") for i in range(2)]
    dwT_stg = [A.take(INTER, f"dwTstg{i}") for i in range(2)]
    A.reset()
    Z0 = qT.off
    def zreg(off, n, name):
        return Region(h_bf, "bf", Z0 + off, n, name, BF_TOTAL)
    dumpC_v = zreg(4096, D, "dumpC_v"); dumpC_s = zreg(4096 + D, D, "dumpC_s")
    xnst = [zreg(8192, D, "xn_a"), zreg(8192 + D, D, "xn_b")]
    dumpA_v = zreg(12288, D, "dumpA_v"); dumpA_s = zreg(12288 + D, D, "dumpA_s")
    wqb = [zreg(21504, D, "wqb0"), zreg(21504 + D, D, "wqb1")]
    dumpB_v = zreg(21504, D, "dumpB_v"); dumpB_s = zreg(21504 + D, D, "dumpB_s")

    stat_sum = Fa.take(24, "sum"); stat_sumsq = Fa.take(24, "sumsq")
    stat_mean = Fa.take(24, "mean"); stat_msq = Fa.take(24, "msq")
    stat_var = Fa.take(24, "var"); stat_lnv = Fa.take(24, "lnv")
    stat_rstd = Fa.take(24, "rstd"); stat_nb = Fa.take(24, "nb")
    c_eps = Fa.take(1, "eps"); c_one = Fa.take(1, "one"); c_qs = Fa.take(1, "qs")
    gs_col = Fa.take(NICH, "gs_col"); us_col = Fa.take(NICH, "us_col")
    ds_col = Fa.take(NCH, "ds_col")
    nmax = Fa.take(2, "nmax"); rsum = Fa.take(2, "rsum"); rcp = Fa.take(2, "rcp")
    sc_row = Fa.take(256, "sc_row")
    out_stg = [Fa.take(D, f"ostg{i}") for i in range(2)]

    pT = [st.enter_context(nc.psum_tensor(f"T{i}", [128, 1024], BF16)) for i in range(2)]
    pA = [st.enter_context(nc.psum_tensor(f"A{i}", [128, 512], F32)) for i in range(2)]
    pS = [st.enter_context(nc.psum_tensor(f"S{i}", [128, 1024], F32)) for i in range(2)]

    def ppg(name):
        return [("psum", name)]

    dma_rr = {"sync": 0, "gpsimd": 0}
    def dma(queue, dst_ap, src_ap, rd, wr, **kw):
        base = "dS" if queue == "sync" else "dG"
        sem = f"{base}{dma_rr[queue] % 6}"
        dma_rr[queue] += 1
        prev = bld.semcnt.get(sem, 0)
        def b(e, cb):
            cb(e.dma_start(dst_ap, src_ap, **kw))
        return bld.op(queue, b, reads=rd, writes=wr, sem=sem, inc=16, is_dma=True,
                      extra_waits=[(sem, prev)] if prev else ())

    def vop(meth, *args, rd=(), wr=(), eng="vector", **kw):
        sem = {"vector": "vS", "scalar": "sS", "gpsimd": "gS"}[eng]
        def b(e, cb):
            cb(getattr(e, meth)(*args, **kw))
        return bld.op(eng, b, reads=rd, writes=wr, sem=sem, inc=1)

    def sevac(dst_ap, src_ap, rd, wr, scale=None):
        if scale is None:
            def b(e, cb):
                cb(e.copy(dst_ap, src_ap))
        else:
            def b(e, cb):
                cb(e.activation(dst_ap, src_ap, AF.Identity, scale=scale))
        return bld.op("scalar", b, reads=rd, writes=wr, sem="sS", inc=1)

    def mm_group(items, wr):
        n = len(items)
        allrd = []
        for _, _, _, rdp in items:
            allrd += list(rdp)
        tok = None
        for i, (l, r, o, rdp) in enumerate(items):
            first, last = i == 0, i == n - 1
            def b(e, cb, l=l, r=r, o=o, first=first, last=last):
                cb(e.matmul(o, l, r, start=first, stop=last))
            tok = bld.op("tensor", b, reads=rdp,
                         writes=wr if last else (),
                         wwaits=wr if first else (),
                         sem="tS", inc=1 if last else 0,
                         record_reads=allrd if last else ())
        return tok

    def tr_batch(srcs, psum_name, psum_ap_fn, ident_ap, rd_extra=()):
        n = len(srcs)
        allrd = []
        for _, rdp in srcs:
            allrd += list(rdp)
        tok = None
        for i, (sap, rdp) in enumerate(srcs):
            def b(e, cb, sap=sap, i=i):
                cb(e.transpose(psum_ap_fn(i), sap, ident_ap))
            tok = bld.op("tensor", b, reads=list(rdp) + [ident_b.pg()],
                         writes=[ppg(psum_name)] if i == n - 1 else (),
                         wwaits=[ppg(psum_name)] if i == 0 else (),
                         sem="tS", inc=1 if i == n - 1 else 0,
                         record_reads=allrd if i == n - 1 else ())
        return tok

    def ln_stats_pass1(src, a, b_, col, dv, ds_):
        vop("tensor_scalar", dv.ap(), src.ap(a, b_), 1.0, 0.0, ALU.mult, ALU.add,
            accum_out=stat_sum.ap(col, col + 1),
            rd=[src.pg(a, b_)], wr=[stat_sum.pg(col, col + 1), dv.pg()])
        vop("activation", ds_.ap(), src.ap(a, b_), AF.Square,
            accum_out=stat_sumsq.ap(col, col + 1), eng="scalar",
            rd=[src.pg(a, b_)], wr=[stat_sumsq.pg(col, col + 1), ds_.pg()])

    def ln_stats_pass2(ntile, base=0):
        a, b_ = base, base + ntile
        vop("tensor_scalar_mul", stat_mean.ap(a, b_), stat_sum.ap(a, b_), 1.0 / D,
            rd=[stat_sum.pg(a, b_)], wr=[stat_mean.pg(a, b_)])
        vop("activation", stat_msq.ap(a, b_), stat_mean.ap(a, b_), AF.Square, eng="scalar",
            rd=[stat_mean.pg(a, b_)], wr=[stat_msq.pg(a, b_)])
        vop("scalar_tensor_tensor", stat_var.ap(a, b_), stat_sumsq.ap(a, b_), 1.0 / D,
            stat_msq.ap(a, b_), ALU.mult, ALU.subtract,
            rd=[stat_sumsq.pg(a, b_), stat_msq.pg(a, b_)], wr=[stat_var.pg(a, b_)])
        vop("activation", stat_lnv.ap(a, b_), stat_var.ap(a, b_), AF.Ln, bias=c_eps.ap(0, 1),
            eng="scalar", rd=[stat_var.pg(a, b_), c_eps.pg()], wr=[stat_lnv.pg(a, b_)])
        vop("activation", stat_rstd.ap(a, b_), stat_lnv.ap(a, b_), AF.Exp, scale=-0.5,
            eng="scalar", rd=[stat_lnv.pg(a, b_)], wr=[stat_rstd.pg(a, b_)])
        vop("scalar_tensor_tensor", stat_nb.ap(a, b_), stat_mean.ap(a, b_), -1.0,
            stat_rstd.ap(a, b_), ALU.mult, ALU.mult,
            rd=[stat_mean.pg(a, b_), stat_rstd.pg(a, b_)], wr=[stat_nb.pg(a, b_)])

    def ln_normalize(src, a, b_, col, dst):
        vop("activation", dst.ap(0, b_ - a), src.ap(a, b_), AF.Identity,
            bias=stat_nb.ap(col, col + 1), scale=stat_rstd.ap(col, col + 1), eng="scalar",
            rd=[src.pg(a, b_), stat_nb.pg(col, col + 1), stat_rstd.pg(col, col + 1)],
            wr=[dst.pg(0, b_ - a)])

    def transpose_tile(src, dstT, tokspan, toff):
        for half in range(2):
            tb = pT[half % 2]
            srcs = [(src.ap((half * 8 + c) * 128, (half * 8 + c + 1) * 128),
                     [src.pg((half * 8 + c) * 128, (half * 8 + c + 1) * 128)])
                    for c in range(8)]
            tr_batch(srcs, f"T{half % 2}", lambda i, tb=tb: tb[:, i * 128:(i + 1) * 128], ident_b.ap())
            dst_ap = dstT.sap(half * 8 * tokspan + toff, [[tokspan, 8], [1, 128]])
            wrpg = []
            for c in range(8):
                wrpg += dstT.pg((half * 8 + c) * tokspan + toff, (half * 8 + c) * tokspan + toff + 128)
            vop("tensor_copy", dst_ap, tb[:, 0:1024].rearrange("p (c n) -> p c n", n=128),
                rd=[ppg(f"T{half % 2}")], wr=[wrpg])

    dma("sync", ident_b.ap(), d_ident, [], [ident_b.pg()])
    vop("memset", c_eps.ap(), EPS, wr=[c_eps.pg()])
    vop("memset", c_one.ap(), 1.0, wr=[c_one.pg()])
    vop("memset", c_qs.ap(), QSCALE, wr=[c_qs.pg()])

    SCALE_UNITS = []

    def scale_cols(src_d, n_elem, dst):
        ncols = n_elem // 128
        done = 0
        while done < ncols:
            take = min(2, ncols - done)
            def unit(done=done, take=take, src_d=src_d, dst=dst):
                dma("sync", sc_row.row0(0, take * 128), src_d[:, done * 128:(done + take) * 128],
                    [], [sc_row.pg(0, 256)])
                srcs = [(sc_row.row0(c * 128, (c + 1) * 128), [sc_row.pg(0, 256), c_one.pg()])
                        for c in range(take)]
                tr_batch(srcs, "A0", lambda i: pA[0][:, i:i + 1], c_one.row0(0, 1))
                vop("tensor_copy", dst.ap(done, done + take), pA[0][:, 0:take],
                    rd=[ppg("A0")], wr=[dst.pg(done, done + take)])
            SCALE_UNITS.append(unit)
            done += take
    scale_cols(d_gs, INTER, gs_col)
    scale_cols(d_us, INTER, us_col)
    scale_cols(d_ds, D, ds_col)

    sB = slotB
    w_stg = [Region(h_bf, "bf", sB.off + k * 2 * D, 2 * D, f"wstg{k}", BF_TOTAL) for k in range(2)]
    wT_stg = [Region(h_bf, "bf", sB.off + 4 * D + k * D, D, f"wTstg{k}", BF_TOTAL) for k in range(2)]
    dw_stg = [Region(h_bf, "bf", sB.off + k * 2816, 2816, f"dwstg{k}", BF_TOTAL) for k in range(2)]
    dwT_stg = [Region(h_bf, "bf", sB.off + 5632 + k * 2816, 2816, f"dwTstg{k}", BF_TOTAL) for k in range(2)]

    PHW_UNITS = []

    def _phw_gu_unit(src_d, gu, i2):
        def unit():
            stg = w_stg[i2 % 2]
            dma("gpsimd", stg.ap().rearrange("p (s h) -> p s h", s=2),
                src_d.rearrange("(s2 s p) h -> p s2 s h", s=2, p=128)[:, i2],
                [], [stg.pg()])
            for s in range(2):
                i = i2 * 2 + s
                tstg = wT_stg[i % 2]
                for half in range(2):
                    par = (i * 2 + half) % 2
                    tb = pT[par]
                    srcs = [(stg.ap(s * D + (half * 8 + c) * 128, s * D + (half * 8 + c + 1) * 128),
                             [stg.pg(s * D + (half * 8 + c) * 128, s * D + (half * 8 + c + 1) * 128)])
                            for c in range(8)]
                    tr_batch(srcs, f"T{par}", lambda i_, tb=tb: tb[:, i_ * 128:(i_ + 1) * 128], ident_b.ap())
                    vop("tensor_copy", tstg.ap(half * 1024, half * 1024 + 1024), tb[:, 0:1024],
                        rd=[ppg(f"T{par}")], wr=[tstg.pg(half * 1024, half * 1024 + 1024)],
                        eng="vector")
                dma("sync", d_guwT[i, gu], tstg.ap(), [tstg.pg()], [[("dram_guw", i * 2 + gu)]])
        return unit

    def _phw_dw_unit(hc, half):
        def unit():
            stg = dw_stg[(hc * 2 + half) % 2]
            dma("gpsimd", stg.ap(),
                d_dw.rearrange("(hc p) i -> hc p i", p=128)[hc][:, half * 2816:(half + 1) * 2816],
                [], [stg.pg()])
            tstg = dwT_stg[(hc * 2 + half) % 2]
            for b2 in range(3):
                hn = min(8, 22 - b2 * 8)
                par = b2 % 2
                tb = pT[par]
                srcs = [(stg.ap((b2 * 8 + c) * 128, (b2 * 8 + c + 1) * 128),
                         [stg.pg((b2 * 8 + c) * 128, (b2 * 8 + c + 1) * 128)])
                        for c in range(hn)]
                tr_batch(srcs, f"T{par}", lambda i_, tb=tb: tb[:, i_ * 128:(i_ + 1) * 128], ident_b.ap())
                vop("tensor_copy", tstg.ap(b2 * 1024, b2 * 1024 + hn * 128), tb[:, 0:hn * 128],
                    rd=[ppg(f"T{par}")], wr=[tstg.pg(b2 * 1024, b2 * 1024 + hn * 128)])
            dma("sync", d_dwT[hc][:, half * 2816:(half + 1) * 2816], tstg.ap(), [tstg.pg()], [[("dram_dw", hc * 2 + half)]])
        return unit

    PHW_UNITS.extend(SCALE_UNITS)
    for i2_ in range(NICH // 2):
        PHW_UNITS.append(_phw_gu_unit(d_gw, 0, i2_))
    for i2_ in range(NICH // 2):
        PHW_UNITS.append(_phw_gu_unit(d_uw, 1, i2_))
    for hc_ in range(NCH):
        for half_ in range(2):
            PHW_UNITS.append(_phw_dw_unit(hc_, half_))

    def pump(k=1):
        for _ in range(k):
            if PHW_UNITS:
                PHW_UNITS.pop(0)()

    def ph_load_x(g, R):
        for t in range(8):
            dma("gpsimd", R.ap(t * D, (t + 1) * D),
                d_x[(g * GT + t * 128):(g * GT + (t + 1) * 128), :],
                [], [R.pg(t * D, (t + 1) * D)])

    def ph_ln1(R, ntile, tokspan):
        for t in range(ntile):
            ln_stats_pass1(R, t * D, (t + 1) * D, t, dv=dumpA_v, ds_=dumpA_s)
        ln_stats_pass2(ntile)
        for t in range(ntile):
            xs = xnst[t % 2]
            ln_normalize(R, t * D, (t + 1) * D, t, xs)
            transpose_tile(xs, featA, tokspan, t * 128)
            pump(1)

    def kv_proj(keyoff, ttiles, tokspan):
        for m in range(4):
            for t in range(ttiles // 4):
                par = (m * 2 + t) % 2
                items = []
                for c in range(NCH):
                    items.append((wkb.ap(c * KVD + m * 128, c * KVD + (m + 1) * 128),
                                  featA.ap(c * tokspan + t * 512, c * tokspan + t * 512 + 512),
                                  pA[par][:, 0:512],
                                  [wkb.pg(c * KVD + m * 128, c * KVD + (m + 1) * 128),
                                   featA.pg(c * tokspan + t * 512, c * tokspan + t * 512 + 512)]))
                mm_group(items, wr=[ppg(f"A{par}")])
                dst_a = m * 1536 + keyoff + t * 512
                sevac(kT.ap(dst_a, dst_a + 512), pA[par][:, 0:512],
                      [ppg(f"A{par}")], [kT.pg(dst_a, dst_a + 512)])
                pump(1)
        for t8 in range(ttiles):
            par = t8 % 2
            items = []
            for c in range(NCH):
                items.append((featA.ap(c * tokspan + t8 * 128, c * tokspan + (t8 + 1) * 128),
                              wvb.ap(c * KVD, (c + 1) * KVD),
                              pS[par][:, 0:512],
                              [featA.pg(c * tokspan + t8 * 128, c * tokspan + (t8 + 1) * 128),
                               wvb.pg(c * KVD, (c + 1) * KVD)]))
            mm_group(items, wr=[ppg(f"S{par}a")])
            koff = keyoff + t8 * 128
            sevac(Vl.ap(koff * 4, koff * 4 + 512), pS[par][:, 0:512],
                  [ppg(f"S{par}a")], [Vl.pg(koff * 4, koff * 4 + 512)])
            pump(1)

    def ph_qkv(g):
        if g > 0:
            dma("gpsimd", wkb.ap().rearrange("p (c m) -> p c m", c=NCH),
                d_wk.rearrange("(c p) m -> p c m", p=128), [], [wkb.pg()])
            dma("gpsimd", wvb.ap().rearrange("p (c m) -> p c m", c=NCH),
                d_wv.rearrange("(c p) m -> p c m", p=128), [], [wvb.pg()])
        kv_proj(512, 8, GT)
        for ms in range(16):
            wq = wqb[ms % 2]
            dma("gpsimd", wq.ap().rearrange("p (c m) -> p c m", c=NCH),
                d_wq.rearrange("(c p) m -> p c m", p=128)[:, :, ms * 128:(ms + 1) * 128],
                [], [wq.pg()])
            for t in range(2):
                par = (ms * 2 + t) % 2
                items = []
                for c in range(NCH):
                    items.append((wq.ap(c * 128, (c + 1) * 128),
                                  featA.ap(c * GT + t * 512, c * GT + t * 512 + 512),
                                  pA[par][:, 0:512],
                                  [wq.pg(c * 128, (c + 1) * 128),
                                   featA.pg(c * GT + t * 512, c * GT + t * 512 + 512)]))
                mm_group(items, wr=[ppg(f"A{par}")])
                sevac(qT.ap(ms * GT + t * 512, ms * GT + t * 512 + 512), pA[par][:, 0:512],
                      [ppg(f"A{par}"), c_qs.pg()],
                      [qT.pg(ms * GT + t * 512, ms * GT + t * 512 + 512)],
                      scale=c_qs.ap(0, 1))
                pump(1)

    def ph_attn(g):
        dma("sync", masks.ap(), d_mask[:, g * 8 * 640:(g + 1) * 8 * 640], [], [masks.pg()])
        iters = [(qt, h) for qt in range(8) for h in range(H)]
        n = len(iters)

        def st_S(j):
            qt, h = iters[j]
            par = j % 2
            hkv = h // 4
            Sp = pS[par]
            lhs = qT.ap(h * GT + qt * 128, h * GT + (qt + 1) * 128)
            lpg = qT.pg(h * GT + qt * 128, h * GT + (qt + 1) * 128)
            kbase = hkv * 1536 + qt * 128
            items = [(lhs, kT.ap(kbase, kbase + 512), Sp[:, 0:512],
                      [lpg, kT.pg(kbase, kbase + 512)]),
                     (lhs, kT.ap(kbase + 512, kbase + 640), Sp[:, 512:640],
                      [kT.pg(kbase + 512, kbase + 640)])]
            allrd = list(items[0][3]) + list(items[1][3])
            for i, (l, r, o, rdp) in enumerate(items):
                def b(e, cb, l=l, r=r, o=o):
                    cb(e.matmul(o, l, r, start=True, stop=True))
                Spg = [("psum", f"S{par}a"), ("psum", f"S{par}b")]
                bld.op("tensor", b, reads=rdp,
                       writes=[Spg] if i == 1 else (),
                       wwaits=[Spg] if i == 0 else (),
                       sem="tS", inc=1 if i == 1 else 0,
                       record_reads=allrd if i == 1 else ())

        def st_max(j):
            par = j % 2
            Spg = [("psum", f"S{par}a"), ("psum", f"S{par}b")]
            vop("reduce_max", nmax.ap(par, par + 1), pS[par][:, 0:640], axis=AX.X,
                negate=True, rd=[Spg], wr=[nmax.pg(par, par + 1)])

        def st_exp(j):
            par = j % 2
            Spg = [("psum", f"S{par}a"), ("psum", f"S{par}b")]
            vop("activation", p_st[par].ap(), pS[par][:, 0:640], AF.Exp,
                bias=nmax.ap(par, par + 1), eng="scalar",
                rd=[Spg, nmax.pg(par, par + 1)], wr=[p_st[par].pg()])

        def st_msk(j):
            qt, h = iters[j]
            par = j % 2
            vop("scalar_tensor_tensor", p_m[par].ap(), p_st[par].ap(), 1.0,
                masks.ap(qt * 640, (qt + 1) * 640), ALU.mult, ALU.mult,
                accum_out=rsum.ap(par, par + 1),
                rd=[p_st[par].pg(), masks.pg(qt * 640, (qt + 1) * 640)],
                wr=[p_m[par].pg(), rsum.pg(par, par + 1)])
            vop("reciprocal", rcp.ap(par, par + 1), rsum.ap(par, par + 1),
                rd=[rsum.pg(par, par + 1)], wr=[rcp.pg(par, par + 1)])
            vop("tensor_scalar_mul", p_n[par].ap(), p_m[par].ap(), rcp.ap(par, par + 1),
                rd=[p_m[par].pg(), rcp.pg(par, par + 1)], wr=[p_n[par].pg()])

        def st_pt(j):
            par = j % 2
            srcs = [(p_n[par].ap(c * 128, (c + 1) * 128), [p_n[par].pg(c * 128, (c + 1) * 128)])
                    for c in range(5)]
            tr_batch(srcs, f"T{par}", lambda i, tb=pT[par]: tb[:, i * 128:(i + 1) * 128],
                     ident_b.ap())
            sevac(ptk[par].ap(), pT[par][:, 0:640], [ppg(f"T{par}")], [ptk[par].pg()])

        def st_pv(j):
            qt, h = iters[j]
            par = j % 2
            hkv = h // 4
            items = []
            for c in range(5):
                vofs = (qt + c) * 512 + hkv * 128
                items.append((Vl.ap(vofs, vofs + 128),
                              ptk[par].ap(c * 128, (c + 1) * 128),
                              pA[par][:, 0:128],
                              [Vl.pg(vofs, vofs + 128), ptk[par].pg(c * 128, (c + 1) * 128)]))
            mm_group(items, wr=[ppg(f"A{par}")])
            sevac(featA.ap(h * GT + qt * 128, h * GT + (qt + 1) * 128),
                  pA[par][:, 0:128], [ppg(f"A{par}")],
                  [featA.pg(h * GT + qt * 128, h * GT + (qt + 1) * 128)])

        for j in range(n + 5):
            pump(1)
            if 0 <= j - 5 < n:
                st_pv(j - 5)
            if 0 <= j - 4 < n:
                st_pt(j - 4)
            if 0 <= j - 3 < n:
                st_msk(j - 3)
            if 0 <= j - 2 < n:
                st_exp(j - 2)
            if 0 <= j - 1 < n:
                st_max(j - 1)
            if j < n:
                st_S(j)

    def ph_wo_ln2(g, R, R2):
        LB = 8
        for hs in range(16):
            wo = wob[hs % 2]
            dma("gpsimd", wo.ap().rearrange("p (c m) -> p c m", c=NCH),
                d_wo.rearrange("(c p) m -> p c m", p=128)[:, :, hs * 128:(hs + 1) * 128],
                [], [wo.pg()])
            for t in range(2):
                par = (hs * 2 + t) % 2
                items = []
                for c in range(NCH):
                    items.append((wo.ap(c * 128, (c + 1) * 128),
                                  featA.ap(c * GT + t * 512, c * GT + t * 512 + 512),
                                  pA[par][:, 0:512],
                                  [wo.pg(c * 128, (c + 1) * 128),
                                   featA.pg(c * GT + t * 512, c * GT + t * 512 + 512)]))
                mm_group(items, wr=[ppg(f"A{par}")])
                yt = ytr[par]
                vop("tensor_copy", yt.ap(), pA[par][:, 0:512],
                    rd=[ppg(f"A{par}")], wr=[yt.pg()])
                srcs = [(yt.ap(i * 128, (i + 1) * 128), [yt.pg(i * 128, (i + 1) * 128)])
                        for i in range(4)]
                tr_batch(srcs, f"T{par}", lambda i, tb=pT[par]: tb[:, i * 128:(i + 1) * 128],
                         ident_b.ap())
                dst = ybuf.sap(t * 4 * D + hs * 128, [[D, 4], [1, 128]])
                wrpg = []
                for i in range(4):
                    wrpg += ybuf.pg((t * 4 + i) * D + hs * 128, (t * 4 + i) * D + (hs + 1) * 128)
                sevac(dst, pT[par][:, 0:512].rearrange("p (c n) -> p c n", n=128),
                      [ppg(f"T{par}")], [wrpg])
        for t in range(8):
            ln_stats_pass1(ybuf, t * D, (t + 1) * D, LB + t, dv=dumpB_v, ds_=dumpB_s)
        ln_stats_pass2(8, base=LB)
        for t in range(8):
            yn = ynst[t % 2]
            ln_normalize(ybuf, t * D, (t + 1) * D, LB + t, yn)
            vop("tensor_add", R2.ap(t * D, (t + 1) * D), R.ap(t * D, (t + 1) * D), yn.ap(),
                eng="gpsimd",
                rd=[R.pg(t * D, (t + 1) * D), yn.pg()], wr=[R2.pg(t * D, (t + 1) * D)])

    fw2 = [Region(h_bf, "bf", fw_stg[0].off, 2 * D, "fw2_0", BF_TOTAL),
           Region(h_bf, "bf", fw_stg[2].off, 2 * D, "fw2_1", BF_TOTAL),
           Region(h_bf, "bf", wqb[0].off, 2 * D, "fw2_2", BF_TOTAL)]

    def ph_ffn(g, R2, dump=False, d_dbg=None, g_=0):
        y2off = NCH * 512

        def ln3(sgi):
            for t4 in range(4):
                t = sgi * 4 + t4
                ln_stats_pass1(R2, t * D, (t + 1) * D, 16 + t4, dv=dumpC_v, ds_=dumpC_s)
            ln_stats_pass2(4, base=16)
            for t4 in range(4):
                t = sgi * 4 + t4
                xs = xnst[t4 % 2]
                ln_normalize(R2, t * D, (t + 1) * D, 16 + t4, xs)
                transpose_tile(xs, featA, 512, t4 * 128)

        def gu(sgi):
            for i in range(NICH):
                par = i % 2
                fw = fw2[i % 2]
                dma("sync", fw.ap(0, 2 * D).rearrange("p (s h) -> p s h", s=2),
                    d_guwT[i].rearrange("s p h -> p s h"),
                    [[("dram_guw", i * 2), ("dram_guw", i * 2 + 1)]], [fw.pg()])
                ig = []
                iu = []
                for c in range(NCH):
                    ig.append((fw.ap(c * 128, (c + 1) * 128),
                               featA.ap(c * 512, c * 512 + 512), pA[par][:, 0:512],
                               [fw.pg(c * 128, (c + 1) * 128), featA.pg(c * 512, c * 512 + 512)]))
                    iu.append((fw.ap(D + c * 128, D + (c + 1) * 128),
                               featA.ap(c * 512, c * 512 + 512), pS[par][:, 0:512],
                               [fw.pg(D + c * 128, D + (c + 1) * 128), featA.pg(c * 512, c * 512 + 512)]))
                mm_group(ig, wr=[ppg(f"A{par}")])
                mm_group(iu, wr=[ppg(f"S{par}a")])
                sgs = sgsb[par]
                vop("activation", sgs.ap(), pA[par][:, 0:512], AF.Silu, scale=gs_col.ap(i, i + 1),
                    eng="scalar", rd=[ppg(f"A{par}"), gs_col.pg(i, i + 1)], wr=[sgs.pg()])
                vop("scalar_tensor_tensor", hdnT.ap(i * 512, (i + 1) * 512), pS[par][:, 0:512],
                    us_col.ap(i, i + 1), sgs.ap(), ALU.mult, ALU.mult,
                    rd=[ppg(f"S{par}a"), us_col.pg(i, i + 1), sgs.pg()],
                    wr=[hdnT.pg(i * 512, (i + 1) * 512)])

        def down(sgi):
            for hc in range(NCH):
                par = hc % 2
                dwl = dwl_stg[par]
                dma("sync", dwl.ap(), d_dwT[hc], [[("dram_dw", hc * 2), ("dram_dw", hc * 2 + 1)]], [dwl.pg()])
                items = []
                for c in range(NICH):
                    items.append((dwl.ap(c * 128, (c + 1) * 128),
                                  hdnT.ap(c * 512, (c + 1) * 512),
                                  pA[par][:, 0:512],
                                  [dwl.pg(c * 128, (c + 1) * 128), hdnT.pg(c * 512, (c + 1) * 512)]))
                mm_group(items, wr=[ppg(f"A{par}")])
                yt = ytr[par]
                sevac(yt.ap(), pA[par][:, 0:512],
                      [ppg(f"A{par}"), ds_col.pg(hc, hc + 1)], [yt.pg()],
                      scale=ds_col.ap(hc, hc + 1))
                srcs = [(yt.ap(i * 128, (i + 1) * 128), [yt.pg(i * 128, (i + 1) * 128)])
                        for i in range(4)]
                tr_batch(srcs, f"T{par}", lambda i, tb=pT[par]: tb[:, i * 128:(i + 1) * 128],
                         ident_b.ap())
                dst = featA.sap(y2off + hc * 128, [[D, 4], [1, 128]])
                wrpg = []
                for i in range(4):
                    wrpg += featA.pg(y2off + i * D + hc * 128, y2off + i * D + (hc + 1) * 128)
                vop("tensor_copy", dst, pT[par][:, 0:512].rearrange("p (c n) -> p c n", n=128),
                    rd=[ppg(f"T{par}")], wr=[wrpg])
            if dump and sgi == 0:
                dma("sync", d_dbg[f"hdnT{g_}"], hdnT.ap(), [hdnT.pg()], [])
                dma("sync", d_dbg[f"y2{g_}"], featA.ap(y2off, y2off + 4 * D),
                    [featA.pg(y2off, y2off + 4 * D)], [])

        def epi(sgi):
            for t4 in range(4):
                ln_stats_pass1(featA, y2off + t4 * D, y2off + (t4 + 1) * D, 20 + t4,
                               dv=dump4_v, ds_=dump4_s)
            ln_stats_pass2(4, base=20)
            for t4 in range(4):
                t = sgi * 4 + t4
                yn = ynst[t4 % 2]
                ln_normalize(featA, y2off + t4 * D, y2off + (t4 + 1) * D, 20 + t4, yn)
                ost = out_stg[t4 % 2]
                vop("tensor_add", ost.ap(), R2.ap(t * D, (t + 1) * D), yn.ap(),
                    eng="gpsimd",
                    rd=[R2.pg(t * D, (t + 1) * D), yn.pg()], wr=[ost.pg()])
                dma("gpsimd", d_out[(g * GT + t * 128):(g * GT + (t + 1) * 128), :], ost.ap(),
                    [ost.pg()], [])

        ln3(0)
        gu(0)
        down(0)
        ln3(1)
        epi(0)
        gu(1)
        down(1)
        epi(1)

    dma("gpsimd", wkb.ap().rearrange("p (c m) -> p c m", c=NCH),
        d_wk.rearrange("(c p) m -> p c m", p=128), [], [wkb.pg()])
    dma("gpsimd", wvb.ap().rearrange("p (c m) -> p c m", c=NCH),
        d_wv.rearrange("(c p) m -> p c m", p=128), [], [wvb.pg()])
    for t in range(4):
        dma("gpsimd", slotA.ap(t * D, (t + 1) * D), d_xh[t * 128:(t + 1) * 128, :],
            [], [slotA.pg(t * D, (t + 1) * D)])
    ph_ln1(slotA, 4, WIN)
    kv_proj(0, 4, WIN)

    for g in range(NG):
        R, R2 = slotA, slotB
        if g > 0:
            for m in range(4):
                vop("tensor_copy", kT.ap(m * 1536, m * 1536 + 512),
                    kT.ap(m * 1536 + 1024, m * 1536 + 1536),
                    rd=[kT.pg(m * 1536 + 1024, m * 1536 + 1536)],
                    wr=[kT.pg(m * 1536, m * 1536 + 512)])
            vop("tensor_copy", Vl.ap(0, 2048), Vl.ap(4096, 6144),
                rd=[Vl.pg(4096, 6144)], wr=[Vl.pg(0, 2048)])
        if g == 0:
            ph_load_x(g, R)
        ph_ln1(R, 8, GT)
        if DEBUG:
            dma("sync", d_dbg[f"xnT{g}"], featA.ap(), [featA.pg()], [])
        ph_qkv(g)
        if DEBUG:
            dma("sync", d_dbg[f"kT{g}"], kT.ap(), [kT.pg()], [])
            dma("sync", d_dbg[f"Vl{g}"], Vl.ap(), [Vl.pg()], [])
            dma("sync", d_dbg[f"qT{g}"], qT.ap(), [qT.pg()], [])
        pump(len(PHW_UNITS))
        ph_attn(g)
        pump(len(PHW_UNITS))
        if DEBUG:
            dma("sync", d_dbg[f"att{g}"], featA.ap(), [featA.pg()], [])
        ph_wo_ln2(g, R, R2)
        if DEBUG:
            dma("sync", d_dbg[f"ybuf{g}"], ybuf.ap(), [ybuf.pg()], [])
            dma("sync", d_dbg[f"R2{g}"], R2.ap(), [R2.pg()], [])
        if g + 1 < NG:
            ph_load_x(g + 1, slotA)
        ph_ffn(g, R2, dump=DEBUG, d_dbg=d_dbg, g_=g)

    final_waits = {s: v for s, v in bld.semcnt.items()
                   if s.startswith("dG") or s.startswith("dS")}
    def runfin(e, semmap):
        for s, v in sorted(final_waits.items()):
            e.wait_ge(semmap[s], v)
    bld.streams["sync"].append(runfin)

    with st:
        semctx = contextlib.ExitStack()
        semmap = {}
        names = set(bld.semcnt) | {"tS", "vS", "sS"}
        for name in sorted(names):
            semmap[name] = semctx.enter_context(nc.semaphore(name))
        with semctx:
            with nc.Block() as block:
                @block.sync
                def _(e):
                    for f in bld.streams["sync"]:
                        f(e, semmap)
                @block.gpsimd
                def _(e):
                    for f in bld.streams["gpsimd"]:
                        f(e, semmap)
                @block.tensor
                def _(e):
                    for f in bld.streams["tensor"]:
                        f(e, semmap)
                @block.vector
                def _(e):
                    for f in bld.streams["vector"]:
                        f(e, semmap)
                @block.scalar
                def _(e):
                    for f in bld.streams["scalar"]:
                        f(e, semmap)
    return nc


def _build_masks():
    out = []
    qq = np.arange(128)[:, None]
    s = np.arange(640)[None, :]
    base = (s > qq) & (s <= 512 + qq)
    for c in range(NCORE):
        seq_off = (c % 4) * TOK
        m = np.zeros((128, 16 * 640), dtype=np.float32)
        for qt in range(16):
            key_seq = seq_off + qt * 128 + s - 512
            ok = base & (key_seq >= 0)
            m[:, qt * 640:(qt + 1) * 640] = ok
        out.append(m.astype(BF))
    return out


_cache = {}


def kernel(L0_attn_x, L0_attn_Wq, L0_attn_Wk, L0_attn_Wv, L0_attn_Wo,
           L0_ffn_gate_w, L0_ffn_up_w, L0_ffn_down_w,
           L0_ffn_gate_s, L0_ffn_up_s, L0_ffn_down_s):
    x = np.ascontiguousarray(np.asarray(L0_attn_x, dtype=np.float32))
    if "nc" not in _cache:
        _cache["nc"] = build_program()
        _cache["masks"] = _build_masks()
        _cache["ident"] = np.eye(128).astype(BF)
    nc = _cache["nc"]
    masks = _cache["masks"]
    ident = _cache["ident"]

    Wq = np.ascontiguousarray(np.asarray(L0_attn_Wq, np.float32))
    Wk = np.ascontiguousarray(np.asarray(L0_attn_Wk, np.float32))
    Wv = np.ascontiguousarray(np.asarray(L0_attn_Wv, np.float32))
    Wo = np.ascontiguousarray(np.asarray(L0_attn_Wo, np.float32))
    gw = np.ascontiguousarray(np.asarray(L0_ffn_gate_w, np.float32))
    uw = np.ascontiguousarray(np.asarray(L0_ffn_up_w, np.float32))
    dw = np.ascontiguousarray(np.asarray(L0_ffn_down_w, np.float32))
    gs = np.ascontiguousarray(np.asarray(L0_ffn_gate_s, np.float32)).reshape(1, -1)
    us = np.ascontiguousarray(np.asarray(L0_ffn_up_s, np.float32)).reshape(1, -1)
    ds = np.ascontiguousarray(np.asarray(L0_ffn_down_s, np.float32)).reshape(1, -1)

    in_maps = []
    for c in range(NCORE):
        b, w = c // 4, c % 4
        xs = np.ascontiguousarray(x[b, w * TOK:(w + 1) * TOK, :])
        if w == 0:
            xh = np.zeros((WIN, D), np.float32)
        else:
            xh = np.ascontiguousarray(x[b, w * TOK - WIN:w * TOK, :])
        in_maps.append({
            "x": xs, "xh": xh, "mask": masks[c], "ident": ident,
            "Wq": Wq, "Wk": Wk, "Wv": Wv, "Wo": Wo,
            "gw": gw, "uw": uw, "dw": dw, "gs": gs, "us": us, "ds": ds,
        })
    res = run_bass_kernel_spmd(nc, in_maps, core_ids=list(range(NCORE)))
    out = np.empty((B, T, D), np.float32)
    for c in range(NCORE):
        b, w = c // 4, c % 4
        out[b, w * TOK:(w + 1) * TOK, :] = res.results[c]["out"]
    if DEBUG:
        _cache["dbg"] = res.results
    return out


# version 17
# speedup vs baseline: 1.2819x; 1.0082x over previous
import contextlib
import numpy as np
import ml_dtypes

import concourse.bass as bass
import concourse.mybir as mybir
from concourse.bass_utils import run_bass_kernel_spmd

F32 = mybir.dt.float32
BF16 = mybir.dt.bfloat16
AF = mybir.ActivationFunctionType
ALU = mybir.AluOpType
AX = mybir.AxisListType
BF = ml_dtypes.bfloat16

B, T, D = 2, 8192, 2048
NCORE = 8
TOK = 2048
WIN = 512
HK, NREP, HD = 4, 4, 128
H = HK * NREP
KVD = HK * HD
INTER = 5632
NCH = D // 128
NICH = INTER // 128
GT = 1024
NG = TOK // GT
SG = 512
NSG = GT // SG
QSCALE = float(1.0 / np.sqrt(HD))
EPS = 1e-5

ENGS = ("sync", "gpsimd", "tensor", "vector", "scalar")
SPACING = 6
PAGE = 128


class Tok:
    __slots__ = ("eng", "idx", "sem", "val", "is_dma")
    def __init__(self, eng, idx, sem, val, is_dma=False):
        self.eng, self.idx, self.sem, self.val, self.is_dma = eng, idx, sem, val, is_dma


class Page:
    __slots__ = ("w", "r")
    def __init__(self):
        self.w = None
        self.r = []


class Builder:
    def __init__(self):
        self.streams = {e: [] for e in ENGS}
        self.icount = {e: 0 for e in ENGS}
        self.semcnt = {}
        self.maxseen = {e: {} for e in ENGS}
        self.pages = {}

    def sem_next(self, name, n):
        v = self.semcnt.get(name, 0) + n
        self.semcnt[name] = v
        return v

    def page(self, key):
        p = self.pages.get(key)
        if p is None:
            p = Page()
            self.pages[key] = p
        return p

    def _want(self, eng, tok, waits):
        if tok is None:
            return
        if tok.eng == eng and not tok.is_dma:
            if self.icount[eng] - tok.idx >= SPACING:
                return
        if self.maxseen[eng].get(tok.sem, 0) >= tok.val:
            return
        if tok.val > waits.get(tok.sem, 0):
            waits[tok.sem] = tok.val

    def op(self, eng, build, reads=(), writes=(), wwaits=(), sem=None, inc=0,
           is_dma=False, record_reads=(), extra_waits=()):
        waits = {}
        for pk in reads:
            for k in pk:
                self._want(eng, self.page(k).w, waits)
        for pk in list(writes) + list(wwaits):
            for k in pk:
                p = self.page(k)
                self._want(eng, p.w, waits)
                for t in p.r:
                    self._want(eng, t, waits)
        for s, v in extra_waits:
            if self.maxseen[eng].get(s, 0) < v and v > waits.get(s, 0):
                waits[s] = v
        if writes:
            assert inc > 0, "writing op must carry a semaphore increment"

        myidx = self.icount[eng]
        tok = None
        if inc:
            val = self.sem_next(sem, inc)
            tok = Tok(eng, myidx, sem, val, is_dma=is_dma)

        wlist = sorted(waits.items())
        for s, v in wlist:
            if self.maxseen[eng].get(s, 0) < v:
                self.maxseen[eng][s] = v
        semname = sem

        def run(e, semmap):
            for s, v in wlist:
                e.wait_ge(semmap[s], v)
            holder = []
            build(e, holder.append)
            assert len(holder) == 1
            if tok is not None:
                holder[0].then_inc(semmap[semname], inc)

        self.streams[eng].append(run)
        self.icount[eng] += 1
        for pk in writes:
            for k in pk:
                p = self.page(k)
                p.w = tok
                p.r = []
        if tok is not None:
            for pk in list(reads) + list(record_reads):
                for k in pk:
                    self.page(k).r.append(tok)
        return tok


class Region:
    __slots__ = ("handle", "off", "n", "space", "name", "ncol")
    def __init__(self, handle, space, off, n, name, ncol):
        self.handle, self.space, self.off, self.n = handle, space, off, n
        self.name, self.ncol = name, ncol

    def ap(self, a=0, b=None):
        if b is None:
            b = self.n
        return self.handle[:, self.off + a:self.off + b]

    def row0(self, a=0, b=None):
        if b is None:
            b = self.n
        return self.handle[0:1, self.off + a:self.off + b]

    def sap(self, elem_off, dims):
        return bass.AP(self.handle, self.off + elem_off, [[self.ncol, 128]] + dims)

    def pg(self, a=0, b=None):
        if b is None:
            b = self.n
        lo = (self.off + a) // PAGE
        hi = (self.off + b - 1) // PAGE
        return [(self.space, i) for i in range(lo, hi + 1)]


class Alloc:
    def __init__(self, handle, space, total, round_to=PAGE):
        self.handle, self.space, self.total = handle, space, total
        self.round_to = round_to
        self.off = 0
        self.marks = []
        self.peak = 0

    def take(self, n, name):
        r_ = self.round_to
        n = ((n + r_ - 1) // r_) * r_
        r = Region(self.handle, self.space, self.off, n, name, self.total)
        self.off += n
        self.peak = max(self.peak, self.off)
        assert self.off <= self.total, f"{self.space} overflow at {name}: {self.off} > {self.total}"
        return r

    def mark(self):
        self.marks.append(self.off)

    def reset(self):
        self.off = self.marks.pop()


DEBUG = False

def build_program():
    nc = bass.Bass("TRN2", target_bir_lowering=False)

    d_x = nc.dram_tensor("x", [TOK, D], F32, kind="ExternalInput").ap()
    d_xh = nc.dram_tensor("xh", [WIN, D], F32, kind="ExternalInput").ap()
    d_mask = nc.dram_tensor("mask", [128, 16 * 640], BF16, kind="ExternalInput").ap()
    d_ident = nc.dram_tensor("ident", [128, 128], BF16, kind="ExternalInput").ap()
    d_wq = nc.dram_tensor("Wq", [D, D], F32, kind="ExternalInput").ap()
    d_wk = nc.dram_tensor("Wk", [D, KVD], F32, kind="ExternalInput").ap()
    d_wv = nc.dram_tensor("Wv", [D, KVD], F32, kind="ExternalInput").ap()
    d_wo = nc.dram_tensor("Wo", [D, D], F32, kind="ExternalInput").ap()
    d_gw = nc.dram_tensor("gw", [INTER, D], F32, kind="ExternalInput").ap()
    d_uw = nc.dram_tensor("uw", [INTER, D], F32, kind="ExternalInput").ap()
    d_dw = nc.dram_tensor("dw", [D, INTER], F32, kind="ExternalInput").ap()
    d_gs = nc.dram_tensor("gs", [1, INTER], F32, kind="ExternalInput").ap()
    d_us = nc.dram_tensor("us", [1, INTER], F32, kind="ExternalInput").ap()
    d_ds = nc.dram_tensor("ds", [1, D], F32, kind="ExternalInput").ap()
    d_out = nc.dram_tensor("out", [TOK, D], F32, kind="ExternalOutput").ap()
    d_dbg = {}
    if DEBUG:
        for g_ in range(2):
            for nm, n in [("xnT", NCH * GT), ("kT", 4 * 1536), ("Vl", 12 * KVD),
                          ("qT", H * GT), ("att", NCH * GT), ("ybuf", 8 * D),
                          ("R2", 8 * D), ("hdnT", NICH * SG), ("y2", 4 * D)]:
                d_dbg[f"{nm}{g_}"] = nc.dram_tensor(f"dbg_{nm}{g_}", [128, n], BF16, kind="ExternalOutput").ap()
    d_guwT = nc.dram_tensor("guwT_s", [NICH, 2, 128, D], BF16).ap()
    d_dwT = nc.dram_tensor("dwT_s", [NCH, 128, NICH * 128], BF16).ap()

    bld = Builder()
    st = contextlib.ExitStack()

    BF_TOTAL = 95360
    F32_TOTAL = 4992
    h_bf = st.enter_context(nc.sbuf_tensor("abf", [128, BF_TOTAL], BF16))
    h_f = st.enter_context(nc.sbuf_tensor("af32", [128, F32_TOTAL], F32))
    A = Alloc(h_bf, "bf", BF_TOTAL)
    Fa = Alloc(h_f, "f32", F32_TOTAL, round_to=32)

    slotA = A.take(8 * D, "slotA")
    slotB = A.take(8 * D, "slotB")
    featA = A.take(NCH * GT, "featA")
    kT = A.take(4 * 1536, "kT")
    Vl = A.take(12 * KVD, "Vl")
    ident_b = A.take(128, "ident")

    A.mark()
    qT = A.take(H * GT, "qT")
    wkb = A.take(NCH * KVD, "wkb")
    wvb = A.take(NCH * KVD, "wvb")
    A.reset(); A.mark()
    hdnT = A.take(NICH * SG, "hdnT")
    fw_stg = [A.take(D, f"fwstg{i}") for i in range(4)]
    A.reset(); A.mark()
    _p1 = A.take(NICH * SG, "_p1")
    dwl_stg = [A.take(NICH * 128, f"dwl{i}") for i in range(2)]
    A.reset(); A.mark()
    _p2 = A.take(H * GT, "_p2")
    masks = A.take(8 * 640, "masks")
    p_st = [A.take(640, f"p_st{i}") for i in range(2)]
    p_m = [A.take(640, f"p_m{i}") for i in range(2)]
    p_n = [A.take(640, f"p_n{i}") for i in range(2)]
    ptk = [A.take(640, f"ptk{i}") for i in range(2)]
    wob = [A.take(NCH * 128, f"wob{i}") for i in range(2)]
    ytr = [A.take(512, f"ytr{i}") for i in range(2)]
    sgsb = [A.take(512, f"sgsb{i}") for i in range(2)]
    A.reset(); A.mark()
    _p2b = A.take(H * GT, "_p2b")
    ynst = [A.take(D, f"ynst{i}") for i in range(2)]
    A.reset(); A.mark()
    dump4_v = A.take(D, "dump4_v")
    dump4_s = A.take(D, "dump4_s")
    A.reset(); A.mark()
    ybuf = A.take(8 * D, "ybuf")
    A.reset(); A.mark()
    w_stg = [A.take(2 * D, f"wstg{i}") for i in range(2)]
    wT_stg = [A.take(D, f"wTstg{i}") for i in range(2)]
    A.reset(); A.mark()
    dw_stg = [A.take(INTER, f"dwstg{i}") for i in range(2)]
    dwT_stg = [A.take(INTER, f"dwTstg{i}") for i in range(2)]
    A.reset()
    Z0 = qT.off
    def zreg(off, n, name):
        return Region(h_bf, "bf", Z0 + off, n, name, BF_TOTAL)
    dumpC_v = zreg(4096, D, "dumpC_v"); dumpC_s = zreg(4096 + D, D, "dumpC_s")
    xnst = [zreg(8192, D, "xn_a"), zreg(8192 + D, D, "xn_b")]
    dumpA_v = zreg(12288, D, "dumpA_v"); dumpA_s = zreg(12288 + D, D, "dumpA_s")
    wqb = [zreg(21504, D, "wqb0"), zreg(21504 + D, D, "wqb1")]
    dumpB_v = zreg(21504, D, "dumpB_v"); dumpB_s = zreg(21504 + D, D, "dumpB_s")

    stat_sum = Fa.take(24, "sum"); stat_sumsq = Fa.take(24, "sumsq")
    stat_mean = Fa.take(24, "mean"); stat_msq = Fa.take(24, "msq")
    stat_var = Fa.take(24, "var"); stat_lnv = Fa.take(24, "lnv")
    stat_rstd = Fa.take(24, "rstd"); stat_nb = Fa.take(24, "nb")
    c_eps = Fa.take(1, "eps"); c_one = Fa.take(1, "one"); c_qs = Fa.take(1, "qs")
    gs_col = Fa.take(NICH, "gs_col"); us_col = Fa.take(NICH, "us_col")
    ds_col = Fa.take(NCH, "ds_col")
    nmax = Fa.take(2, "nmax"); rsum = Fa.take(2, "rsum"); rcp = Fa.take(2, "rcp")
    sc_row = Fa.take(256, "sc_row")
    out_stg = [Fa.take(D, f"ostg{i}") for i in range(2)]

    pT = [st.enter_context(nc.psum_tensor(f"T{i}", [128, 1024], BF16)) for i in range(2)]
    pA = [st.enter_context(nc.psum_tensor(f"A{i}", [128, 512], F32)) for i in range(2)]
    pS = [st.enter_context(nc.psum_tensor(f"S{i}", [128, 1024], F32)) for i in range(2)]

    def ppg(name):
        return [("psum", name)]

    dma_rr = {"sync": 0, "gpsimd": 0}
    def dma(queue, dst_ap, src_ap, rd, wr, **kw):
        base = "dS" if queue == "sync" else "dG"
        sem = f"{base}{dma_rr[queue] % 6}"
        dma_rr[queue] += 1
        prev = bld.semcnt.get(sem, 0)
        def b(e, cb):
            cb(e.dma_start(dst_ap, src_ap, **kw))
        return bld.op(queue, b, reads=rd, writes=wr, sem=sem, inc=16, is_dma=True,
                      extra_waits=[(sem, prev)] if prev else ())

    def vop(meth, *args, rd=(), wr=(), eng="vector", **kw):
        sem = {"vector": "vS", "scalar": "sS", "gpsimd": "gS"}[eng]
        def b(e, cb):
            cb(getattr(e, meth)(*args, **kw))
        return bld.op(eng, b, reads=rd, writes=wr, sem=sem, inc=1)

    def sevac(dst_ap, src_ap, rd, wr, scale=None):
        if scale is None:
            def b(e, cb):
                cb(e.copy(dst_ap, src_ap))
        else:
            def b(e, cb):
                cb(e.activation(dst_ap, src_ap, AF.Identity, scale=scale))
        return bld.op("scalar", b, reads=rd, writes=wr, sem="sS", inc=1)

    def mm_group(items, wr):
        n = len(items)
        allrd = []
        for _, _, _, rdp in items:
            allrd += list(rdp)
        tok = None
        for i, (l, r, o, rdp) in enumerate(items):
            first, last = i == 0, i == n - 1
            def b(e, cb, l=l, r=r, o=o, first=first, last=last):
                cb(e.matmul(o, l, r, start=first, stop=last))
            tok = bld.op("tensor", b, reads=rdp,
                         writes=wr if last else (),
                         wwaits=wr if first else (),
                         sem="tS", inc=1 if last else 0,
                         record_reads=allrd if last else ())
        return tok

    def tr_batch(srcs, psum_name, psum_ap_fn, ident_ap, rd_extra=()):
        n = len(srcs)
        allrd = []
        for _, rdp in srcs:
            allrd += list(rdp)
        tok = None
        for i, (sap, rdp) in enumerate(srcs):
            def b(e, cb, sap=sap, i=i):
                cb(e.transpose(psum_ap_fn(i), sap, ident_ap))
            tok = bld.op("tensor", b, reads=list(rdp) + [ident_b.pg()],
                         writes=[ppg(psum_name)] if i == n - 1 else (),
                         wwaits=[ppg(psum_name)] if i == 0 else (),
                         sem="tS", inc=1 if i == n - 1 else 0,
                         record_reads=allrd if i == n - 1 else ())
        return tok

    def ln_stats_pass1(src, a, b_, col, dv, ds_):
        vop("tensor_scalar", dv.ap(), src.ap(a, b_), 1.0, 0.0, ALU.mult, ALU.add,
            accum_out=stat_sum.ap(col, col + 1),
            rd=[src.pg(a, b_)], wr=[stat_sum.pg(col, col + 1), dv.pg()])
        vop("activation", ds_.ap(), src.ap(a, b_), AF.Square,
            accum_out=stat_sumsq.ap(col, col + 1), eng="scalar",
            rd=[src.pg(a, b_)], wr=[stat_sumsq.pg(col, col + 1), ds_.pg()])

    def ln_stats_pass2(ntile, base=0):
        a, b_ = base, base + ntile
        vop("tensor_scalar_mul", stat_mean.ap(a, b_), stat_sum.ap(a, b_), 1.0 / D,
            rd=[stat_sum.pg(a, b_)], wr=[stat_mean.pg(a, b_)])
        vop("activation", stat_msq.ap(a, b_), stat_mean.ap(a, b_), AF.Square, eng="scalar",
            rd=[stat_mean.pg(a, b_)], wr=[stat_msq.pg(a, b_)])
        vop("scalar_tensor_tensor", stat_var.ap(a, b_), stat_sumsq.ap(a, b_), 1.0 / D,
            stat_msq.ap(a, b_), ALU.mult, ALU.subtract,
            rd=[stat_sumsq.pg(a, b_), stat_msq.pg(a, b_)], wr=[stat_var.pg(a, b_)])
        vop("activation", stat_lnv.ap(a, b_), stat_var.ap(a, b_), AF.Ln, bias=c_eps.ap(0, 1),
            eng="scalar", rd=[stat_var.pg(a, b_), c_eps.pg()], wr=[stat_lnv.pg(a, b_)])
        vop("activation", stat_rstd.ap(a, b_), stat_lnv.ap(a, b_), AF.Exp, scale=-0.5,
            eng="scalar", rd=[stat_lnv.pg(a, b_)], wr=[stat_rstd.pg(a, b_)])
        vop("scalar_tensor_tensor", stat_nb.ap(a, b_), stat_mean.ap(a, b_), -1.0,
            stat_rstd.ap(a, b_), ALU.mult, ALU.mult,
            rd=[stat_mean.pg(a, b_), stat_rstd.pg(a, b_)], wr=[stat_nb.pg(a, b_)])

    def ln_normalize(src, a, b_, col, dst):
        vop("activation", dst.ap(0, b_ - a), src.ap(a, b_), AF.Identity,
            bias=stat_nb.ap(col, col + 1), scale=stat_rstd.ap(col, col + 1), eng="scalar",
            rd=[src.pg(a, b_), stat_nb.pg(col, col + 1), stat_rstd.pg(col, col + 1)],
            wr=[dst.pg(0, b_ - a)])

    def transpose_tile(src, dstT, tokspan, toff):
        for half in range(2):
            tb = pT[half % 2]
            srcs = [(src.ap((half * 8 + c) * 128, (half * 8 + c + 1) * 128),
                     [src.pg((half * 8 + c) * 128, (half * 8 + c + 1) * 128)])
                    for c in range(8)]
            tr_batch(srcs, f"T{half % 2}", lambda i, tb=tb: tb[:, i * 128:(i + 1) * 128], ident_b.ap())
            dst_ap = dstT.sap(half * 8 * tokspan + toff, [[tokspan, 8], [1, 128]])
            wrpg = []
            for c in range(8):
                wrpg += dstT.pg((half * 8 + c) * tokspan + toff, (half * 8 + c) * tokspan + toff + 128)
            vop("tensor_copy", dst_ap, tb[:, 0:1024].rearrange("p (c n) -> p c n", n=128),
                rd=[ppg(f"T{half % 2}")], wr=[wrpg])

    dma("sync", ident_b.ap(), d_ident, [], [ident_b.pg()])
    vop("memset", c_eps.ap(), EPS, wr=[c_eps.pg()])
    vop("memset", c_one.ap(), 1.0, wr=[c_one.pg()])
    vop("memset", c_qs.ap(), QSCALE, wr=[c_qs.pg()])

    SCALE_UNITS = []

    def scale_cols(src_d, n_elem, dst):
        ncols = n_elem // 128
        done = 0
        while done < ncols:
            take = min(2, ncols - done)
            def unit(done=done, take=take, src_d=src_d, dst=dst):
                dma("sync", sc_row.row0(0, take * 128), src_d[:, done * 128:(done + take) * 128],
                    [], [sc_row.pg(0, 256)])
                srcs = [(sc_row.row0(c * 128, (c + 1) * 128), [sc_row.pg(0, 256), c_one.pg()])
                        for c in range(take)]
                tr_batch(srcs, "A0", lambda i: pA[0][:, i:i + 1], c_one.row0(0, 1))
                vop("tensor_copy", dst.ap(done, done + take), pA[0][:, 0:take],
                    rd=[ppg("A0")], wr=[dst.pg(done, done + take)])
            SCALE_UNITS.append(unit)
            done += take
    scale_cols(d_gs, INTER, gs_col)
    scale_cols(d_us, INTER, us_col)
    scale_cols(d_ds, D, ds_col)

    sB = slotB
    w_stg = [Region(h_bf, "bf", sB.off + k * 2 * D, 2 * D, f"wstg{k}", BF_TOTAL) for k in range(2)]
    wT_stg = [Region(h_bf, "bf", sB.off + 4 * D + k * D, D, f"wTstg{k}", BF_TOTAL) for k in range(2)]
    dw_stg = [Region(h_bf, "bf", sB.off + k * 2816, 2816, f"dwstg{k}", BF_TOTAL) for k in range(2)]
    dwT_stg = [Region(h_bf, "bf", sB.off + 5632 + k * 2816, 2816, f"dwTstg{k}", BF_TOTAL) for k in range(2)]

    PHW_UNITS = []

    def _phw_gu_unit(src_d, gu, i2):
        def unit():
            stg = w_stg[i2 % 2]
            dma("gpsimd", stg.ap().rearrange("p (s h) -> p s h", s=2),
                src_d.rearrange("(s2 s p) h -> p s2 s h", s=2, p=128)[:, i2],
                [], [stg.pg()])
            for s in range(2):
                i = i2 * 2 + s
                tstg = wT_stg[i % 2]
                for half in range(2):
                    par = (i * 2 + half) % 2
                    tb = pT[par]
                    srcs = [(stg.ap(s * D + (half * 8 + c) * 128, s * D + (half * 8 + c + 1) * 128),
                             [stg.pg(s * D + (half * 8 + c) * 128, s * D + (half * 8 + c + 1) * 128)])
                            for c in range(8)]
                    tr_batch(srcs, f"T{par}", lambda i_, tb=tb: tb[:, i_ * 128:(i_ + 1) * 128], ident_b.ap())
                    vop("tensor_copy", tstg.ap(half * 1024, half * 1024 + 1024), tb[:, 0:1024],
                        rd=[ppg(f"T{par}")], wr=[tstg.pg(half * 1024, half * 1024 + 1024)],
                        eng="vector")
                dma("sync", d_guwT[i, gu], tstg.ap(), [tstg.pg()], [[("dram_guw", i * 2 + gu)]])
        return unit

    def _phw_dw_unit(hc, half):
        def unit():
            stg = dw_stg[(hc * 2 + half) % 2]
            dma("gpsimd", stg.ap(),
                d_dw.rearrange("(hc p) i -> hc p i", p=128)[hc][:, half * 2816:(half + 1) * 2816],
                [], [stg.pg()])
            tstg = dwT_stg[(hc * 2 + half) % 2]
            for b2 in range(3):
                hn = min(8, 22 - b2 * 8)
                par = b2 % 2
                tb = pT[par]
                srcs = [(stg.ap((b2 * 8 + c) * 128, (b2 * 8 + c + 1) * 128),
                         [stg.pg((b2 * 8 + c) * 128, (b2 * 8 + c + 1) * 128)])
                        for c in range(hn)]
                tr_batch(srcs, f"T{par}", lambda i_, tb=tb: tb[:, i_ * 128:(i_ + 1) * 128], ident_b.ap())
                vop("tensor_copy", tstg.ap(b2 * 1024, b2 * 1024 + hn * 128), tb[:, 0:hn * 128],
                    rd=[ppg(f"T{par}")], wr=[tstg.pg(b2 * 1024, b2 * 1024 + hn * 128)])
            dma("sync", d_dwT[hc][:, half * 2816:(half + 1) * 2816], tstg.ap(), [tstg.pg()], [[("dram_dw", hc * 2 + half)]])
        return unit

    PHW_UNITS.extend(SCALE_UNITS)
    for i2_ in range(NICH // 2):
        PHW_UNITS.append(_phw_gu_unit(d_gw, 0, i2_))
    for i2_ in range(NICH // 2):
        PHW_UNITS.append(_phw_gu_unit(d_uw, 1, i2_))
    for hc_ in range(NCH):
        for half_ in range(2):
            PHW_UNITS.append(_phw_dw_unit(hc_, half_))

    def pump(k=1):
        for _ in range(k):
            if PHW_UNITS:
                PHW_UNITS.pop(0)()

    def ph_load_x(g, R):
        for t in range(8):
            dma("gpsimd", R.ap(t * D, (t + 1) * D),
                d_x[(g * GT + t * 128):(g * GT + (t + 1) * 128), :],
                [], [R.pg(t * D, (t + 1) * D)])

    def ph_ln1(R, ntile, tokspan):
        half = 4 if ntile == 8 else ntile
        for t0 in range(0, ntile, half):
            for t in range(t0, t0 + half):
                ln_stats_pass1(R, t * D, (t + 1) * D, t, dv=dumpA_v, ds_=dumpA_s)
            ln_stats_pass2(half, base=t0)
            for t in range(t0, t0 + half):
                xs = xnst[t % 2]
                ln_normalize(R, t * D, (t + 1) * D, t, xs)
                transpose_tile(xs, featA, tokspan, t * 128)
                pump(1)

    def kv_proj(keyoff, ttiles, tokspan):
        for m in range(4):
            for t in range(ttiles // 4):
                par = (m * 2 + t) % 2
                items = []
                for c in range(NCH):
                    items.append((wkb.ap(c * KVD + m * 128, c * KVD + (m + 1) * 128),
                                  featA.ap(c * tokspan + t * 512, c * tokspan + t * 512 + 512),
                                  pA[par][:, 0:512],
                                  [wkb.pg(c * KVD + m * 128, c * KVD + (m + 1) * 128),
                                   featA.pg(c * tokspan + t * 512, c * tokspan + t * 512 + 512)]))
                mm_group(items, wr=[ppg(f"A{par}")])
                dst_a = m * 1536 + keyoff + t * 512
                sevac(kT.ap(dst_a, dst_a + 512), pA[par][:, 0:512],
                      [ppg(f"A{par}")], [kT.pg(dst_a, dst_a + 512)])
                pump(1)
        for t8 in range(ttiles):
            par = t8 % 2
            items = []
            for c in range(NCH):
                items.append((featA.ap(c * tokspan + t8 * 128, c * tokspan + (t8 + 1) * 128),
                              wvb.ap(c * KVD, (c + 1) * KVD),
                              pS[par][:, 0:512],
                              [featA.pg(c * tokspan + t8 * 128, c * tokspan + (t8 + 1) * 128),
                               wvb.pg(c * KVD, (c + 1) * KVD)]))
            mm_group(items, wr=[ppg(f"S{par}a")])
            koff = keyoff + t8 * 128
            sevac(Vl.ap(koff * 4, koff * 4 + 512), pS[par][:, 0:512],
                  [ppg(f"S{par}a")], [Vl.pg(koff * 4, koff * 4 + 512)])
            pump(1)

    def ph_qkv(g):
        if g > 0:
            dma("gpsimd", wkb.ap().rearrange("p (c m) -> p c m", c=NCH),
                d_wk.rearrange("(c p) m -> p c m", p=128), [], [wkb.pg()])
            dma("gpsimd", wvb.ap().rearrange("p (c m) -> p c m", c=NCH),
                d_wv.rearrange("(c p) m -> p c m", p=128), [], [wvb.pg()])
        kv_proj(512, 8, GT)
        for ms in range(16):
            wq = wqb[ms % 2]
            dma("gpsimd", wq.ap().rearrange("p (c m) -> p c m", c=NCH),
                d_wq.rearrange("(c p) m -> p c m", p=128)[:, :, ms * 128:(ms + 1) * 128],
                [], [wq.pg()])
            for t in range(2):
                par = (ms * 2 + t) % 2
                items = []
                for c in range(NCH):
                    items.append((wq.ap(c * 128, (c + 1) * 128),
                                  featA.ap(c * GT + t * 512, c * GT + t * 512 + 512),
                                  pA[par][:, 0:512],
                                  [wq.pg(c * 128, (c + 1) * 128),
                                   featA.pg(c * GT + t * 512, c * GT + t * 512 + 512)]))
                mm_group(items, wr=[ppg(f"A{par}")])
                sevac(qT.ap(ms * GT + t * 512, ms * GT + t * 512 + 512), pA[par][:, 0:512],
                      [ppg(f"A{par}"), c_qs.pg()],
                      [qT.pg(ms * GT + t * 512, ms * GT + t * 512 + 512)],
                      scale=c_qs.ap(0, 1))
                pump(1)

    def ph_attn(g):
        dma("sync", masks.ap(), d_mask[:, g * 8 * 640:(g + 1) * 8 * 640], [], [masks.pg()])
        iters = [(qt, h) for qt in range(8) for h in range(H)]
        n = len(iters)

        def st_S(j):
            qt, h = iters[j]
            par = j % 2
            hkv = h // 4
            Sp = pS[par]
            lhs = qT.ap(h * GT + qt * 128, h * GT + (qt + 1) * 128)
            lpg = qT.pg(h * GT + qt * 128, h * GT + (qt + 1) * 128)
            kbase = hkv * 1536 + qt * 128
            items = [(lhs, kT.ap(kbase, kbase + 512), Sp[:, 0:512],
                      [lpg, kT.pg(kbase, kbase + 512)]),
                     (lhs, kT.ap(kbase + 512, kbase + 640), Sp[:, 512:640],
                      [kT.pg(kbase + 512, kbase + 640)])]
            allrd = list(items[0][3]) + list(items[1][3])
            for i, (l, r, o, rdp) in enumerate(items):
                def b(e, cb, l=l, r=r, o=o):
                    cb(e.matmul(o, l, r, start=True, stop=True))
                Spg = [("psum", f"S{par}a"), ("psum", f"S{par}b")]
                bld.op("tensor", b, reads=rdp,
                       writes=[Spg] if i == 1 else (),
                       wwaits=[Spg] if i == 0 else (),
                       sem="tS", inc=1 if i == 1 else 0,
                       record_reads=allrd if i == 1 else ())

        def st_max(j):
            par = j % 2
            Spg = [("psum", f"S{par}a"), ("psum", f"S{par}b")]
            vop("reduce_max", nmax.ap(par, par + 1), pS[par][:, 0:640], axis=AX.X,
                negate=True, rd=[Spg], wr=[nmax.pg(par, par + 1)])

        def st_exp(j):
            par = j % 2
            Spg = [("psum", f"S{par}a"), ("psum", f"S{par}b")]
            vop("activation", p_st[par].ap(), pS[par][:, 0:640], AF.Exp,
                bias=nmax.ap(par, par + 1), eng="scalar",
                rd=[Spg, nmax.pg(par, par + 1)], wr=[p_st[par].pg()])

        def st_msk(j):
            qt, h = iters[j]
            par = j % 2
            vop("scalar_tensor_tensor", p_m[par].ap(), p_st[par].ap(), 1.0,
                masks.ap(qt * 640, (qt + 1) * 640), ALU.mult, ALU.mult,
                accum_out=rsum.ap(par, par + 1),
                rd=[p_st[par].pg(), masks.pg(qt * 640, (qt + 1) * 640)],
                wr=[p_m[par].pg(), rsum.pg(par, par + 1)])
            vop("reciprocal", rcp.ap(par, par + 1), rsum.ap(par, par + 1),
                rd=[rsum.pg(par, par + 1)], wr=[rcp.pg(par, par + 1)])
            vop("tensor_scalar_mul", p_n[par].ap(), p_m[par].ap(), rcp.ap(par, par + 1),
                rd=[p_m[par].pg(), rcp.pg(par, par + 1)], wr=[p_n[par].pg()])

        def st_pt(j):
            par = j % 2
            srcs = [(p_n[par].ap(c * 128, (c + 1) * 128), [p_n[par].pg(c * 128, (c + 1) * 128)])
                    for c in range(5)]
            tr_batch(srcs, f"T{par}", lambda i, tb=pT[par]: tb[:, i * 128:(i + 1) * 128],
                     ident_b.ap())
            sevac(ptk[par].ap(), pT[par][:, 0:640], [ppg(f"T{par}")], [ptk[par].pg()])

        def st_pv(j):
            qt, h = iters[j]
            par = j % 2
            hkv = h // 4
            items = []
            for c in range(5):
                vofs = (qt + c) * 512 + hkv * 128
                items.append((Vl.ap(vofs, vofs + 128),
                              ptk[par].ap(c * 128, (c + 1) * 128),
                              pA[par][:, 0:128],
                              [Vl.pg(vofs, vofs + 128), ptk[par].pg(c * 128, (c + 1) * 128)]))
            mm_group(items, wr=[ppg(f"A{par}")])
            sevac(featA.ap(h * GT + qt * 128, h * GT + (qt + 1) * 128),
                  pA[par][:, 0:128], [ppg(f"A{par}")],
                  [featA.pg(h * GT + qt * 128, h * GT + (qt + 1) * 128)])

        for j in range(n + 5):
            pump(1)
            if 0 <= j - 5 < n:
                st_pv(j - 5)
            if 0 <= j - 4 < n:
                st_pt(j - 4)
            if 0 <= j - 3 < n:
                st_msk(j - 3)
            if 0 <= j - 2 < n:
                st_exp(j - 2)
            if 0 <= j - 1 < n:
                st_max(j - 1)
            if j < n:
                st_S(j)

    def ph_wo_ln2(g, R, R2):
        LB = 8
        for hs in range(16):
            wo = wob[hs % 2]
            dma("gpsimd", wo.ap().rearrange("p (c m) -> p c m", c=NCH),
                d_wo.rearrange("(c p) m -> p c m", p=128)[:, :, hs * 128:(hs + 1) * 128],
                [], [wo.pg()])
            for t in range(2):
                par = (hs * 2 + t) % 2
                items = []
                for c in range(NCH):
                    items.append((wo.ap(c * 128, (c + 1) * 128),
                                  featA.ap(c * GT + t * 512, c * GT + t * 512 + 512),
                                  pA[par][:, 0:512],
                                  [wo.pg(c * 128, (c + 1) * 128),
                                   featA.pg(c * GT + t * 512, c * GT + t * 512 + 512)]))
                mm_group(items, wr=[ppg(f"A{par}")])
                yt = ytr[par]
                vop("tensor_copy", yt.ap(), pA[par][:, 0:512],
                    rd=[ppg(f"A{par}")], wr=[yt.pg()])
                srcs = [(yt.ap(i * 128, (i + 1) * 128), [yt.pg(i * 128, (i + 1) * 128)])
                        for i in range(4)]
                tr_batch(srcs, f"T{par}", lambda i, tb=pT[par]: tb[:, i * 128:(i + 1) * 128],
                         ident_b.ap())
                dst = ybuf.sap(t * 4 * D + hs * 128, [[D, 4], [1, 128]])
                wrpg = []
                for i in range(4):
                    wrpg += ybuf.pg((t * 4 + i) * D + hs * 128, (t * 4 + i) * D + (hs + 1) * 128)
                sevac(dst, pT[par][:, 0:512].rearrange("p (c n) -> p c n", n=128),
                      [ppg(f"T{par}")], [wrpg])
        for t in range(8):
            ln_stats_pass1(ybuf, t * D, (t + 1) * D, LB + t, dv=dumpB_v, ds_=dumpB_s)
        ln_stats_pass2(8, base=LB)
        for t in range(8):
            yn = ynst[t % 2]
            ln_normalize(ybuf, t * D, (t + 1) * D, LB + t, yn)
            vop("tensor_add", R2.ap(t * D, (t + 1) * D), R.ap(t * D, (t + 1) * D), yn.ap(),
                eng="gpsimd",
                rd=[R.pg(t * D, (t + 1) * D), yn.pg()], wr=[R2.pg(t * D, (t + 1) * D)])

    fw2 = [Region(h_bf, "bf", fw_stg[0].off, 2 * D, "fw2_0", BF_TOTAL),
           Region(h_bf, "bf", fw_stg[2].off, 2 * D, "fw2_1", BF_TOTAL),
           Region(h_bf, "bf", wqb[0].off, 2 * D, "fw2_2", BF_TOTAL)]

    def ph_ffn(g, R2, dump=False, d_dbg=None, g_=0):
        y2off = NCH * 512

        def ln3(sgi):
            for t4 in range(4):
                t = sgi * 4 + t4
                ln_stats_pass1(R2, t * D, (t + 1) * D, 16 + t4, dv=dumpC_v, ds_=dumpC_s)
            ln_stats_pass2(4, base=16)
            for t4 in range(4):
                t = sgi * 4 + t4
                xs = xnst[t4 % 2]
                ln_normalize(R2, t * D, (t + 1) * D, 16 + t4, xs)
                transpose_tile(xs, featA, 512, t4 * 128)

        def issue_fw(i):
            fw = fw2[i % 2]
            dma("sync", fw.ap(0, 2 * D).rearrange("p (s h) -> p s h", s=2),
                d_guwT[i].rearrange("s p h -> p s h"),
                [[("dram_guw", i * 2), ("dram_guw", i * 2 + 1)]], [fw.pg()])

        def gu(sgi, prefetched=0):
            for i in range(NICH):
                par = i % 2
                fw = fw2[i % 2]
                if i >= prefetched:
                    issue_fw(i)
                ig = []
                iu = []
                for c in range(NCH):
                    ig.append((fw.ap(c * 128, (c + 1) * 128),
                               featA.ap(c * 512, c * 512 + 512), pA[par][:, 0:512],
                               [fw.pg(c * 128, (c + 1) * 128), featA.pg(c * 512, c * 512 + 512)]))
                    iu.append((fw.ap(D + c * 128, D + (c + 1) * 128),
                               featA.ap(c * 512, c * 512 + 512), pS[par][:, 0:512],
                               [fw.pg(D + c * 128, D + (c + 1) * 128), featA.pg(c * 512, c * 512 + 512)]))
                mm_group(ig, wr=[ppg(f"A{par}")])
                mm_group(iu, wr=[ppg(f"S{par}a")])
                sgs = sgsb[par]
                vop("activation", sgs.ap(), pA[par][:, 0:512], AF.Silu, scale=gs_col.ap(i, i + 1),
                    eng="scalar", rd=[ppg(f"A{par}"), gs_col.pg(i, i + 1)], wr=[sgs.pg()])
                vop("scalar_tensor_tensor", hdnT.ap(i * 512, (i + 1) * 512), pS[par][:, 0:512],
                    us_col.ap(i, i + 1), sgs.ap(), ALU.mult, ALU.mult,
                    rd=[ppg(f"S{par}a"), us_col.pg(i, i + 1), sgs.pg()],
                    wr=[hdnT.pg(i * 512, (i + 1) * 512)])

        def issue_dwl(hc):
            dwl = dwl_stg[hc % 2]
            dma("sync", dwl.ap(), d_dwT[hc],
                [[("dram_dw", hc * 2), ("dram_dw", hc * 2 + 1)]], [dwl.pg()])

        def down(sgi, prefetched=0):
            for hc in range(NCH):
                par = hc % 2
                dwl = dwl_stg[par]
                if hc >= prefetched:
                    issue_dwl(hc)
                items = []
                for c in range(NICH):
                    items.append((dwl.ap(c * 128, (c + 1) * 128),
                                  hdnT.ap(c * 512, (c + 1) * 512),
                                  pA[par][:, 0:512],
                                  [dwl.pg(c * 128, (c + 1) * 128), hdnT.pg(c * 512, (c + 1) * 512)]))
                mm_group(items, wr=[ppg(f"A{par}")])
                yt = ytr[par]
                sevac(yt.ap(), pA[par][:, 0:512],
                      [ppg(f"A{par}"), ds_col.pg(hc, hc + 1)], [yt.pg()],
                      scale=ds_col.ap(hc, hc + 1))
                srcs = [(yt.ap(i * 128, (i + 1) * 128), [yt.pg(i * 128, (i + 1) * 128)])
                        for i in range(4)]
                tr_batch(srcs, f"T{par}", lambda i, tb=pT[par]: tb[:, i * 128:(i + 1) * 128],
                         ident_b.ap())
                dst = featA.sap(y2off + hc * 128, [[D, 4], [1, 128]])
                wrpg = []
                for i in range(4):
                    wrpg += featA.pg(y2off + i * D + hc * 128, y2off + i * D + (hc + 1) * 128)
                vop("tensor_copy", dst, pT[par][:, 0:512].rearrange("p (c n) -> p c n", n=128),
                    rd=[ppg(f"T{par}")], wr=[wrpg])
            if dump and sgi == 0:
                dma("sync", d_dbg[f"hdnT{g_}"], hdnT.ap(), [hdnT.pg()], [])
                dma("sync", d_dbg[f"y2{g_}"], featA.ap(y2off, y2off + 4 * D),
                    [featA.pg(y2off, y2off + 4 * D)], [])

        def epi(sgi):
            for t4 in range(4):
                ln_stats_pass1(featA, y2off + t4 * D, y2off + (t4 + 1) * D, 20 + t4,
                               dv=dump4_v, ds_=dump4_s)
            ln_stats_pass2(4, base=20)
            for t4 in range(4):
                t = sgi * 4 + t4
                yn = ynst[t4 % 2]
                ln_normalize(featA, y2off + t4 * D, y2off + (t4 + 1) * D, 20 + t4, yn)
                ost = out_stg[t4 % 2]
                vop("tensor_add", ost.ap(), R2.ap(t * D, (t + 1) * D), yn.ap(),
                    eng="gpsimd",
                    rd=[R2.pg(t * D, (t + 1) * D), yn.pg()], wr=[ost.pg()])
                dma("gpsimd", d_out[(g * GT + t * 128):(g * GT + (t + 1) * 128), :], ost.ap(),
                    [ost.pg()], [])

        ln3(0)
        gu(0)
        down(0)
        ln3(1)
        epi(0)
        gu(1)
        down(1)
        epi(1)

    dma("gpsimd", wkb.ap().rearrange("p (c m) -> p c m", c=NCH),
        d_wk.rearrange("(c p) m -> p c m", p=128), [], [wkb.pg()])
    dma("gpsimd", wvb.ap().rearrange("p (c m) -> p c m", c=NCH),
        d_wv.rearrange("(c p) m -> p c m", p=128), [], [wvb.pg()])
    for t in range(4):
        dma("gpsimd", slotA.ap(t * D, (t + 1) * D), d_xh[t * 128:(t + 1) * 128, :],
            [], [slotA.pg(t * D, (t + 1) * D)])
    ph_ln1(slotA, 4, WIN)
    kv_proj(0, 4, WIN)

    for g in range(NG):
        R, R2 = slotA, slotB
        if g > 0:
            for m in range(4):
                vop("tensor_copy", kT.ap(m * 1536, m * 1536 + 512),
                    kT.ap(m * 1536 + 1024, m * 1536 + 1536),
                    rd=[kT.pg(m * 1536 + 1024, m * 1536 + 1536)],
                    wr=[kT.pg(m * 1536, m * 1536 + 512)])
            vop("tensor_copy", Vl.ap(0, 2048), Vl.ap(4096, 6144),
                rd=[Vl.pg(4096, 6144)], wr=[Vl.pg(0, 2048)])
        if g == 0:
            ph_load_x(g, R)
        ph_ln1(R, 8, GT)
        if DEBUG:
            dma("sync", d_dbg[f"xnT{g}"], featA.ap(), [featA.pg()], [])
        ph_qkv(g)
        if DEBUG:
            dma("sync", d_dbg[f"kT{g}"], kT.ap(), [kT.pg()], [])
            dma("sync", d_dbg[f"Vl{g}"], Vl.ap(), [Vl.pg()], [])
            dma("sync", d_dbg[f"qT{g}"], qT.ap(), [qT.pg()], [])
        pump(len(PHW_UNITS))
        ph_attn(g)
        pump(len(PHW_UNITS))
        if DEBUG:
            dma("sync", d_dbg[f"att{g}"], featA.ap(), [featA.pg()], [])
        ph_wo_ln2(g, R, R2)
        if DEBUG:
            dma("sync", d_dbg[f"ybuf{g}"], ybuf.ap(), [ybuf.pg()], [])
            dma("sync", d_dbg[f"R2{g}"], R2.ap(), [R2.pg()], [])
        if g + 1 < NG:
            ph_load_x(g + 1, slotA)
        ph_ffn(g, R2, dump=DEBUG, d_dbg=d_dbg, g_=g)

    final_waits = {s: v for s, v in bld.semcnt.items()
                   if s.startswith("dG") or s.startswith("dS")}
    def runfin(e, semmap):
        for s, v in sorted(final_waits.items()):
            e.wait_ge(semmap[s], v)
    bld.streams["sync"].append(runfin)

    with st:
        semctx = contextlib.ExitStack()
        semmap = {}
        names = set(bld.semcnt) | {"tS", "vS", "sS"}
        for name in sorted(names):
            semmap[name] = semctx.enter_context(nc.semaphore(name))
        with semctx:
            with nc.Block() as block:
                @block.sync
                def _(e):
                    for f in bld.streams["sync"]:
                        f(e, semmap)
                @block.gpsimd
                def _(e):
                    for f in bld.streams["gpsimd"]:
                        f(e, semmap)
                @block.tensor
                def _(e):
                    for f in bld.streams["tensor"]:
                        f(e, semmap)
                @block.vector
                def _(e):
                    for f in bld.streams["vector"]:
                        f(e, semmap)
                @block.scalar
                def _(e):
                    for f in bld.streams["scalar"]:
                        f(e, semmap)
    return nc


def _build_masks():
    out = []
    qq = np.arange(128)[:, None]
    s = np.arange(640)[None, :]
    base = (s > qq) & (s <= 512 + qq)
    for c in range(NCORE):
        seq_off = (c % 4) * TOK
        m = np.zeros((128, 16 * 640), dtype=np.float32)
        for qt in range(16):
            key_seq = seq_off + qt * 128 + s - 512
            ok = base & (key_seq >= 0)
            m[:, qt * 640:(qt + 1) * 640] = ok
        out.append(m.astype(BF))
    return out


_cache = {}


def kernel(L0_attn_x, L0_attn_Wq, L0_attn_Wk, L0_attn_Wv, L0_attn_Wo,
           L0_ffn_gate_w, L0_ffn_up_w, L0_ffn_down_w,
           L0_ffn_gate_s, L0_ffn_up_s, L0_ffn_down_s):
    x = np.ascontiguousarray(np.asarray(L0_attn_x, dtype=np.float32))
    if "nc" not in _cache:
        _cache["nc"] = build_program()
        _cache["masks"] = _build_masks()
        _cache["ident"] = np.eye(128).astype(BF)
    nc = _cache["nc"]
    masks = _cache["masks"]
    ident = _cache["ident"]

    Wq = np.ascontiguousarray(np.asarray(L0_attn_Wq, np.float32))
    Wk = np.ascontiguousarray(np.asarray(L0_attn_Wk, np.float32))
    Wv = np.ascontiguousarray(np.asarray(L0_attn_Wv, np.float32))
    Wo = np.ascontiguousarray(np.asarray(L0_attn_Wo, np.float32))
    gw = np.ascontiguousarray(np.asarray(L0_ffn_gate_w, np.float32))
    uw = np.ascontiguousarray(np.asarray(L0_ffn_up_w, np.float32))
    dw = np.ascontiguousarray(np.asarray(L0_ffn_down_w, np.float32))
    gs = np.ascontiguousarray(np.asarray(L0_ffn_gate_s, np.float32)).reshape(1, -1)
    us = np.ascontiguousarray(np.asarray(L0_ffn_up_s, np.float32)).reshape(1, -1)
    ds = np.ascontiguousarray(np.asarray(L0_ffn_down_s, np.float32)).reshape(1, -1)

    in_maps = []
    for c in range(NCORE):
        b, w = c // 4, c % 4
        xs = np.ascontiguousarray(x[b, w * TOK:(w + 1) * TOK, :])
        if w == 0:
            xh = np.zeros((WIN, D), np.float32)
        else:
            xh = np.ascontiguousarray(x[b, w * TOK - WIN:w * TOK, :])
        in_maps.append({
            "x": xs, "xh": xh, "mask": masks[c], "ident": ident,
            "Wq": Wq, "Wk": Wk, "Wv": Wv, "Wo": Wo,
            "gw": gw, "uw": uw, "dw": dw, "gs": gs, "us": us, "ds": ds,
        })
    res = run_bass_kernel_spmd(nc, in_maps, core_ids=list(range(NCORE)))
    out = np.empty((B, T, D), np.float32)
    for c in range(NCORE):
        b, w = c // 4, c % 4
        out[b, w * TOK:(w + 1) * TOK, :] = res.results[c]["out"]
    if DEBUG:
        _cache["dbg"] = res.results
    return out
